# version 34
# baseline (speedup 1.0000x reference)
import os
import numpy as np
import ml_dtypes
import concourse.bass as bass
import concourse.mybir as mybir
from concourse.bass_utils import run_bass_kernel_spmd

F32 = mybir.dt.float32
BF16 = mybir.dt.bfloat16
AF = mybir.ActivationFunctionType
ALU = mybir.AluOpType

L, D, H, DH = 4096, 1024, 16, 64
C = 128
TSB = 512
NSB = L // TSB
CPS = TSB // C
NCH = L // C
LEAD = 16
W = TSB + LEAD
NT = 8
OF = 512
TF = 512
WF = TF + LEAD
nf32 = np.float32


class Sem:
    def __init__(self, name):
        self.name, self.cnt, self.h = name, 0, None


STOP = int(os.environ.get("KSTOP", "99"))
KCHUNK = int(os.environ.get("KCHUNK", "99"))
KSUB = float(os.environ.get("KSUB", "99"))
KDRAIN = int(os.environ.get("KDRAIN", "1"))


class Builder:
    def __init__(self):
        self.steps = []
        self.sems = {}
        self.stopped = False

    def sem(self, name):
        s = Sem(name)
        self.sems[name] = s
        return s

    def emit(self, eng, fn, waits=(), incs=()):
        if self.stopped:
            return
        for s, d in incs:
            s.cnt += d
        self.steps.append((eng, [(s, v) for s, v in waits if v > 0], fn, list(incs)))

    def mm(self, fn, sbase, obase, waits=(), incs=()):
        if self.stopped:
            return
        if (sbase, obase) != getattr(self, "pe_base", (0, 0)):
            self.steps.append(("pe", [], lambda e: e.drain(), []))
            self.pe_base = (sbase, obase)
        self.emit("pe", fn, waits, incs)

    def dma(self, queue, sem, out, in_, waits=(), transpose=False):
        if transpose:
            fn = lambda e, o=out, i=in_: e.dma_start_transpose(o, i)
        else:
            fn = lambda e, o=out, i=in_: e.dma_start(out=o, in_=i)
        self.emit(queue, fn, waits=waits, incs=[(sem, 16)])
        return sem.cnt


def build_nc(stop=99):
    nc = bass.Bass("TRN2", target_bir_lowering=False)
    B = Builder()

    x_d = nc.dram_tensor("x", [L, D], F32, kind="ExternalInput").ap()
    xh_d = nc.dram_tensor("xh", [L, OF], F32, kind="ExternalInput").ap()
    wq_d = nc.dram_tensor("Wq", [D, D], F32, kind="ExternalInput").ap()
    wk1_d = nc.dram_tensor("Wk1", [D, D], F32, kind="ExternalInput").ap()
    wk2_d = nc.dram_tensor("Wk2", [D, D], F32, kind="ExternalInput").ap()
    wph_d = nc.dram_tensor("WpH", [D, OF], F32, kind="ExternalInput").ap()
    wrth_d = nc.dram_tensor("WrTH", [D, OF], F32, kind="ExternalInput").ap()
    wktf_d = nc.dram_tensor("WkT", [D, D], F32, kind="ExternalInput").ap()
    wvth_d = nc.dram_tensor("WvTH", [D, OF], F32, kind="ExternalInput").ap()
    bq_d = nc.dram_tensor("bq", [D], F32, kind="ExternalInput").ap()
    bk1_d = nc.dram_tensor("bk1", [D], F32, kind="ExternalInput").ap()
    bk2_d = nc.dram_tensor("bk2", [D], F32, kind="ExternalInput").ap()
    bph_d = nc.dram_tensor("bpH", [OF], F32, kind="ExternalInput").ap()
    mur_d = nc.dram_tensor("mu_r", [D], F32, kind="ExternalInput").ap()
    muk_d = nc.dram_tensor("mu_k", [D], F32, kind="ExternalInput").ap()
    wgb_d = nc.dram_tensor("wgb", [D, H], F32, kind="ExternalInput").ap()
    wsb_d = nc.dram_tensor("wsb", [D, H], F32, kind="ExternalInput").ap()
    bgv_d = nc.dram_tensor("bgv", [H, 1], F32, kind="ExternalInput").ap()
    bsv_d = nc.dram_tensor("bsv", [H, 1], F32, kind="ExternalInput").ap()
    mask_d = nc.dram_tensor("maskS", [C, 512], BF16, kind="ExternalInput").ap()
    eg_d = nc.dram_tensor("EGF", [H, NT * C], BF16, kind="ExternalInput").ap()
    ones2_d = nc.dram_tensor("ones2", [2, C], BF16, kind="ExternalInput").ap()
    out_d = nc.dram_tensor("out", [L, OF], F32, kind="ExternalOutput").ap()

    xnd = nc.dram_tensor("xnd", [L + LEAD, D], BF16).ap()
    ktd = nc.dram_tensor("ktd", [D, L], BF16).ap()
    k2d = nc.dram_tensor("k2d", [D, L], BF16).ap()
    od = nc.dram_tensor("Od", [L + LEAD, D], BF16).ap()

    ARENA = 200 * 1024
    arena_h = nc.alloc_sbuf_tensor("myarena", [128, ARENA], mybir.dt.uint8)
    abase = nc.lookup_mloc(arena_h).addr
    off = [abase]

    def alloc(name, shape, dt):
        per = int(np.prod(shape[1:])) * mybir.dt.size(dt)
        a = (off[0] + 31) // 32 * 32
        hdl = nc.alloc_sbuf_tensor_at(name, list(shape), dt, offset=a)
        off[0] = a + per
        return hdl

    maskS = alloc("maskS_s", [C, 512], BF16)
    eg_s = alloc("eg_s", [H, NT * C], BF16)
    ones2_s = alloc("ones2_s", [2, C], BF16)
    wgb_s = alloc("wgb_s", [C, H * NT], BF16)
    wsb_s = alloc("wsb_s", [C, H * NT], BF16)
    bq_s = alloc("bq_s", [C, NT], F32)
    bk1_s = alloc("bk1_s", [C, NT], F32)
    bk2s_s = alloc("bk2s_s", [C, NT], F32)
    nbq8_s = alloc("nbq8_s", [C, NT], F32)
    mur_s = alloc("mur_s", [C, NT], F32)
    muk_s = alloc("muk_s", [C, NT], F32)
    omr_s = alloc("omr_s", [C, NT], F32)
    omk_s = alloc("omk_s", [C, NT], F32)
    bgv_s = alloc("bgv_s", [H, 1], F32)
    bsv_s = alloc("bsv_s", [H, 1], F32)
    carry = alloc("carry", [H, 1], F32)
    zrow = alloc("zrow", [LEAD, D], BF16)
    bp2 = alloc("bp2", [2, OF], BF16)
    ones16 = alloc("ones16", [H, TSB], F32)
    g_t1 = alloc("g_t1", [H, TSB], F32)
    g_t2 = alloc("g_t2", [H, TSB], F32)
    g_cum = alloc("g_cum", [H, TSB], F32)
    g_a = alloc("g_a", [H, TSB], F32)
    g_r = alloc("g_r", [H, TSB], F32)
    g_gc = g_cum
    g_arr = g_t2
    g_gcbf = alloc("g_gcbf", [H, TSB], BF16)
    g_arrbf = alloc("g_arrbf", [H, TSB], BF16)
    warena = off[0]
    wq_s = alloc("wq_s", [C, NT * D], BF16)
    wk1a = off[0]
    wk1_s = alloc("wk1_s", [C, NT * D], BF16)
    wk2a = off[0]
    wk2_s = alloc("wk2_s", [C, NT * D], BF16)
    wtop = off[0]
    off[0] = wk1a
    wkt_s = alloc("wkt_s", [C, NT * D], BF16)
    off[0] = wk2a
    wrt_s = alloc("wrt_s", [C, NT * OF], BF16)
    wvt_s = alloc("wvt_s", [C, NT * OF], BF16)
    off[0] = warena
    wrA_s = alloc("wrA_s", [C, NT * OF], BF16)
    wrB_s = alloc("wrB_s", [C, NT * OF], BF16)
    off[0] = max(off[0], wtop)
    wp_s = alloc("wp_s", [C, NT * OF], BF16)
    ln1 = alloc("ln1", [C, D], BF16)
    ln2 = alloc("ln2", [C, D], BF16)
    stat_m = alloc("stat_m", [C, CPS], F32)
    stat_s1 = alloc("stat_s1", [C, CPS], F32)
    stat_s2 = alloc("stat_s2", [C, CPS], F32)
    stat_v = alloc("stat_v", [C, CPS], F32)
    stat_r = alloc("stat_r", [C, CPS], F32)
    x_ln = [alloc("x_ln0", [C, D], F32), alloc("x_ln1", [C, D], F32)]
    gcb_s = [alloc("gcb_s0", [C, TSB], BF16), alloc("gcb_s1", [C, TSB], BF16)]
    q2scr = alloc("q2scr", [C, TSB], BF16)
    arrb_s = [alloc("arrb_s0", [C, TSB], BF16), alloc("arrb_s1", [C, TSB], BF16)]
    ktok = [alloc("ktok0", [C, D], BF16), alloc("ktok1", [C, D], BF16)]
    k2tok = [alloc("k2tok0", [C, D], BF16), alloc("k2tok1", [C, D], BF16)]
    e_buf = alloc("e_buf", [C, D], BF16)
    o1sf = alloc("o1sf", [C, D], F32)
    ocomb = [alloc("ocomb0", [C, D], BF16), alloc("ocomb1", [C, D], BF16)]
    pms = [alloc("pms0", [C, D], BF16), alloc("pms1", [C, D], BF16)]
    p2m = [alloc("p2m0", [C, D], BF16), alloc("p2m1", [C, D], BF16)]
    minis = alloc("minis", [1, D], F32)
    s1_sb = alloc("s1_sb", [C, 1024], BF16)
    s2_sb = alloc("s2_sb", [C, 1024], BF16)
    arena = off[0]
    xnT = alloc("xnT", [C, NT * W], BF16)
    qt_s = alloc("qt_s", [C, NT * W], BF16)
    kb_s = alloc("kb_s", [C, NT * W], BF16)
    kt_s = alloc("kt_s", [C, NT * W], BF16)
    k2_s = alloc("k2_s", [C, NT * W], BF16)
    q2_s = alloc("q2_s", [C, NT * W], BF16)
    xtok = [alloc("xtok0", [C, CPS * D], BF16), alloc("xtok1", [C, CPS * D], BF16)]
    top = off[0]
    off[0] = arena
    xnF_b = [alloc("xnF0", [C, NT * WF], BF16),
             alloc("xnF1", [C, NT * WF], BF16)]
    xk_s = alloc("xk_s", [C, NT * TF], BF16)
    ka_s = alloc("ka_s", [C, NT * TF], BF16)
    ot_s = [alloc("ot_s0", [C, NT * TF], BF16), alloc("ot_s1", [C, NT * TF], BF16)]
    kr_s = alloc("kr_s", [C, TF], BF16)
    kr2_s = alloc("kr2_s", [C, TF], BF16)
    r_bf = [alloc("r_bf0", [C, OF], BF16), alloc("r_bf1", [C, OF], BF16)]
    gg1 = [alloc("gg1a", [C, OF], F32), alloc("gg1b", [C, OF], F32)]
    gg2 = [alloc("gg2a", [C, OF], F32), alloc("gg2b", [C, OF], F32)]
    off[0] = max(off[0], top)
    s1_acc = alloc("s1_acc", [C, 1024], F32)
    assert off[0] - abase <= ARENA, f"SBUF over budget: {off[0] - abase}"

    ctxs = []

    def penter(cm):
        ctxs.append(cm)
        return cm.__enter__()

    pA = penter(nc.psum_tensor("pA", [C, 1024], F32))
    pB = penter(nc.psum_tensor("pB", [C, 1024], F32))
    pC = penter(nc.psum_tensor("pC", [C, 1024], F32))
    pD = penter(nc.psum_tensor("pD", [C, 512], F32))
    pE = penter(nc.psum_tensor("pE", [C, 512], F32))

    s_dc = B.sem("d_c"); s_dw = B.sem("d_w")
    s_dx = [B.sem("d_x0"), B.sem("d_x1")]
    s_dxnd = B.sem("d_xnd"); s_dxnt = B.sem("d_xnt"); s_dktd = B.sem("d_ktd")
    s_dktok = [B.sem("d_ktok0"), B.sem("d_ktok1")]
    s_dod = [B.sem("d_od0"), B.sem("d_od1")]
    s_dot = [B.sem("d_ot0"), B.sem("d_ot1")]
    s_dout = [B.sem("d_out0"), B.sem("d_out1")]
    s_dxf = [B.sem("d_xf0"), B.sem("d_xf1")]
    s_dxn = [[B.sem("d_fxn00"), B.sem("d_fxn01")],
             [B.sem("d_fxn10"), B.sem("d_fxn11")]]
    s_pe = B.sem("pe"); s_act = B.sem("act"); s_dve = B.sem("dve")
    s_pool = B.sem("pool")

    SY, GP, PE, AC, DV, PO = "sync", "gpsimd", "pe", "act", "dve", "pool"

    def wsl(tsr, ft, c0, n, r0=0, nr=C):
        return tsr[r0:r0 + nr, W * ft + c0: W * ft + c0 + n]

    B.dma(SY, s_dc, maskS[:], mask_d)
    B.dma(SY, s_dc, eg_s[:], eg_d)
    B.dma(SY, s_dc, ones2_s[:], ones2_d)
    B.dma(SY, s_dc, bgv_s[:], bgv_d)
    B.dma(SY, s_dc, bsv_s[:], bsv_d)
    for ft in range(NT):
        for dst, src in ((bq_s, bq_d), (bk1_s, bk1_d), (bk2s_s, bk2_d),
                         (mur_s, mur_d), (muk_s, muk_d)):
            B.dma(SY, s_dc, dst[:, ft:ft + 1],
                  src[128 * ft:128 * ft + 128].rearrange("(p o) -> p o", o=1))
    c_consts = s_dc.cnt
    B.dma(GP, s_dw, wq_s.ap().rearrange("p (k d) -> p k d", d=D), wq_d.rearrange("(k p) d -> p k d", p=128))
    B.dma(GP, s_dw, wk1_s.ap().rearrange("p (k d) -> p k d", d=D), wk1_d.rearrange("(k p) d -> p k d", p=128))
    B.dma(GP, s_dw, wk2_s.ap().rearrange("p (k d) -> p k d", d=D), wk2_d.rearrange("(k p) d -> p k d", p=128))
    B.dma(GP, s_dw, wgb_s.ap().rearrange("p (f g) -> p f g", g=H), wgb_d.rearrange("(f p) g -> p f g", p=128))
    B.dma(GP, s_dw, wsb_s.ap().rearrange("p (f g) -> p f g", g=H), wsb_d.rearrange("(f p) g -> p f g", p=128))
    c_wqkk = s_dw.cnt
    B.dma(GP, s_dw, wp_s.ap().rearrange("p (k d) -> p k d", d=OF), wph_d.rearrange("(k p) d -> p k d", p=128))
    B.emit(GP, lambda e: e.memset(bp2[:], 0.0))
    B.dma(GP, s_dw, bp2[0:1, :], bph_d.rearrange("(o d) -> o d", o=1))
    c_winit = s_dw.cnt

    wc = [(s_dc, c_consts)]
    B.emit(PO, lambda e: e.memset(zrow[:], 0.0), waits=wc)
    B.emit(PO, lambda e: e.memset(ones16[:], 1.0))
    B.emit(PO, lambda e: e.memset(carry[:], 0.0))
    B.emit(PO, lambda e: e.memset(s1_acc[:], 0.0))
    for t in (qt_s, kb_s, kt_s, k2_s, q2_s, xnT):
        B.emit(PO, lambda e, t=t: e.memset(
            t.ap().rearrange("p (b w) -> p b w", w=W)[:, :, 0:LEAD], 0.0))
    B.emit(PO, lambda e: e.engine_nop(), incs=[(s_pool, 1)])
    c_prep_pool = s_pool.cnt
    B.emit(DV, lambda e: e.tensor_scalar_mul(omr_s[:], mur_s[:], -1.0), waits=wc)
    B.emit(DV, lambda e: e.tensor_scalar_add(omr_s[:], omr_s[:], 1.0))
    B.emit(DV, lambda e: e.tensor_scalar_mul(omk_s[:], muk_s[:], -1.0))
    B.emit(DV, lambda e: e.tensor_scalar_add(omk_s[:], omk_s[:], 1.0))
    B.emit(DV, lambda e: e.tensor_scalar_mul(bk2s_s[:], bk2s_s[:], float(nf32(0.02 / 32.0))))
    B.emit(DV, lambda e: e.tensor_scalar_mul(nbq8_s[:], bq_s[:], -0.125),
           incs=[(s_dve, 1)])
    B.dma(SY, s_dxnd, xnd[0:LEAD, :], zrow[:], waits=[(s_pool, c_prep_pool)])

    rec_pe_done = 0
    proj_pe_done = [0]
    ds1_mark = [0, 0]
    ds2_mark = [0, 0]
    sevac2_mark = [0, 0]
    s1cast_mark = [0, 0]
    pD_free_dve = 0
    ktokbuf_free = {0: 0, 1: 0}
    pA_free_dve = 0
    pB_free_dve = 0
    pE_free_act = 0
    pE_free_dve = 0
    bc_dve_free = [0]
    pD_free_act = 0
    o1sf_free = [(0, 0), (0, 0)]
    pms_free = {0: 0, 1: 0}
    p2m_free = {0: 0, 1: 0}
    od_cnt = {0: 0, 1: 0}
    pend_od = [None]
    xnd_cnt = 0
    c_ktS = 0
    qt_all = 0
    k1_evacs = {}

    xnd_holder = [0]
    wxnt_holder = [[]]

    ln_free = {0: 0, 1: 0}

    lnld = {0: 0, 1: 0}

    def emit_ln_loads2(s2, pair):
        sb0 = s2 * TSB
        for tt in (2 * pair, 2 * pair + 1):
            t0 = sb0 + C * tt
            xb = x_ln[tt % 2]
            xw = [(s_dve, ln_free[tt % 2])] if ln_free[tt % 2] else []
            lnld[tt % 2] = B.dma(SY, s_dx[tt % 2], xb[:], x_d[t0:t0 + C, :],
                                 waits=xw)

    def emit_ln_stats2(s2, pair):
        for tt in (2 * pair, 2 * pair + 1):
            xb = x_ln[tt % 2]
            wcx = [(s_dx[tt % 2], lnld[tt % 2])]
            B.emit(DV, lambda e, xb=xb, tt=tt: e.tensor_scalar(
                ln1[:], xb[:], 1.0, 0.0, ALU.mult, ALU.add,
                accum_out=stat_s1[:, tt:tt + 1]), waits=wcx)
            B.emit(DV, lambda e, xb=xb, tt=tt: e.scalar_tensor_tensor(
                ln2[:], xb[:], 1.0, xb[:], ALU.mult, ALU.mult,
                accum_out=stat_s2[:, tt:tt + 1]))
            B.emit(DV, lambda e, tt=tt: e.tensor_scalar_mul(
                stat_m[:, tt:tt + 1], stat_s1[:, tt:tt + 1], 1.0 / D))
            B.emit(DV, lambda e, tt=tt: e.tensor_scalar_mul(
                stat_v[:, tt:tt + 1], stat_s2[:, tt:tt + 1], 1.0 / D))
            B.emit(DV, lambda e, tt=tt: e.tensor_mul(
                stat_r[:, tt:tt + 1], stat_m[:, tt:tt + 1], stat_m[:, tt:tt + 1]))
            B.emit(DV, lambda e, tt=tt: e.tensor_sub(
                stat_v[:, tt:tt + 1], stat_v[:, tt:tt + 1], stat_r[:, tt:tt + 1]))
            xt2 = xtok[s2 % 2]
            xtw = [(s_pe, rec_pe_done)] if tt == 0 else []
            B.emit(DV, lambda e, xb=xb, xt2=xt2, tt=tt: e.tensor_scalar(
                xt2[:, D * tt:D * tt + D], xb[:], stat_m[:, tt:tt + 1], 1.0,
                ALU.subtract, ALU.mult),
                waits=xtw, incs=[(s_dve, 1)])
            ln_free[tt % 2] = s_dve.cnt

    def emit_ln_rstd_apply(s2):
        sb0 = s2 * TSB
        B.emit(AC, lambda e: e.activation(stat_r[:], stat_v[:], AF.Ln,
                                          bias=nf32(1e-5)),
               waits=[(s_dve, ln_free[0]), (s_dve, ln_free[1])])
        B.emit(AC, lambda e: e.activation(stat_r[:], stat_r[:], AF.Exp,
                                          scale=nf32(-0.5)), incs=[(s_act, 1)])
        cr = s_act.cnt
        xt = xtok[s2 % 2]
        for tt in range(CPS):
            B.emit(DV, lambda e, xt=xt, tt=tt: e.tensor_scalar_mul(
                xt[:, D * tt:D * tt + D], xt[:, D * tt:D * tt + D],
                stat_r[:, tt:tt + 1]),
                waits=[(s_act, cr)] if tt == 0 else [], incs=[(s_dve, 1)])
            cxn = s_dve.cnt
            t0 = sb0 + C * tt
            xnd_holder[0] = B.dma(SY, s_dxnd, xnd[LEAD + t0:LEAD + t0 + C, :],
                                  xt[:, D * tt:D * tt + D], waits=[(s_dve, cxn)])
    def emit_ln_treads(s2):
        sb0 = s2 * TSB
        trw = [(s_dxnd, xnd_holder[0]), (s_pe, proj_pe_done[0]),
               (s_dc, c_consts)]
        for ft in range(NT):
            B.dma(SY, s_dxnt, xnT[:, W * ft:W * ft + W],
                  xnd[sb0:sb0 + W, 128 * ft:128 * ft + 128],
                  waits=trw if ft == 0 else [], transpose=True)
        wxnt_holder[0] = [(s_dxnt, s_dxnt.cnt)]

    emit_ln_loads2(0, 0)
    emit_ln_stats2(0, 0)
    emit_ln_loads2(0, 1)
    emit_ln_stats2(0, 1)
    emit_ln_rstd_apply(0)
    emit_ln_treads(0)
    for s in range(NSB):
        sb0 = s * TSB

        gw = wxnt_holder[0] + [(s_dw, c_winit), (s_act, pD_free_act),
                               (s_dve, pD_free_dve)]
        for ft in range(NT):
            B.mm(lambda e, ft=ft: e.matmul(
                pD[0:H, 0:TSB], wgb_s[:, H * ft:H * ft + H],
                wsl(xnT, ft, LEAD, TSB), start=(ft == 0), stop=(ft == NT - 1)),
                0, 0,
                waits=gw if ft == 0 else [],
                incs=[(s_pe, 1)] if ft == NT - 1 else [])
        c_gpre = s_pe.cnt
        B.emit(AC, lambda e: e.activation(g_t1[:], pD[0:H, 0:TSB], AF.Sigmoid,
                                          bias=bgv_s[:]),
               waits=[(s_pe, c_gpre)], incs=[(s_act, 1)])
        c_g = s_act.cnt
        pD_free_act = c_g

        k1_evacs = {}
        for ft in range(NT):
            ph = pA[:, 512 * (ft % 2):512 * (ft % 2) + 512]
            if ft < 2:
                waits = [(s_dw, c_winit), (s_dve, pA_free_dve)]
            else:
                waits = [(s_act, k1_evacs[ft - 2])]
            for kt in range(NT):
                B.mm(lambda e, ph=ph, kt=kt, ft=ft: e.matmul(
                    ph, wk1_s[:, D * kt + 128 * ft: D * kt + 128 * ft + 128],
                    wsl(xnT, kt, LEAD, TSB), start=(kt == 0), stop=(kt == NT - 1)),
                    0, 0,
                    waits=waits if kt == 0 else [],
                    incs=[(s_pe, 1)] if kt == NT - 1 else [])
            cpe = s_pe.cnt
            B.emit(AC, lambda e, ph=ph, ft=ft: e.activation(
                wsl(kb_s, ft, LEAD, TSB), ph, AF.Identity, bias=bk1_s[:, ft:ft + 1]),
                waits=[(s_pe, cpe)], incs=[(s_act, 1)])
            k1_evacs[ft] = s_act.cnt
        c_kb = s_act.cnt
        c_k1pe = s_pe.cnt

        for ft in range(NT):
            B.mm(lambda e, ft=ft: e.matmul(
                pD[0:H, 0:TSB], wsb_s[:, H * ft:H * ft + H],
                wsl(kb_s, ft, LEAD, TSB), start=(ft == 0), stop=(ft == NT - 1)),
                0, 0,
                waits=[(s_act, c_kb)] if ft == 0 else [],
                incs=[(s_pe, 1)] if ft == NT - 1 else [])
        c_rpre = s_pe.cnt
        B.emit(AC, lambda e: e.activation(g_r[:], pD[0:H, 0:TSB], AF.Sigmoid,
                                          bias=bsv_s[:]),
               waits=[(s_pe, c_rpre)])
        B.emit(AC, lambda e: e.activation(g_t2[:], pD[0:H, 0:TSB], AF.Identity,
                                          bias=bsv_s[:]), incs=[(s_act, 1)])
        ca2 = s_act.cnt
        pD_free_act = ca2
        B.emit(DV, lambda e: e.tensor_mul(g_r[:], g_r[:], g_t2[:]),
               waits=[(s_act, ca2)], incs=[(s_dve, 1)])
        c_rsig = s_dve.cnt

        B.emit(DV, lambda e: e.tensor_scalar_max(g_t1[:], g_t1[:], 1e-6),
               waits=[(s_act, c_g)], incs=[(s_dve, 1)])
        cd = s_dve.cnt
        B.emit(AC, lambda e: e.activation(g_t2[:], g_t1[:], AF.Ln),
               waits=[(s_dve, cd)], incs=[(s_act, 1)])
        ca2 = s_act.cnt
        B.emit(DV, lambda e: e.tensor_tensor_scan(
            g_cum[:], ones16[:], g_t2[:], carry[:, 0:1], ALU.mult, ALU.add),
            waits=[(s_act, ca2), (s_pool, c_prep_pool)])
        B.emit(DV, lambda e: e.tensor_copy(carry[:], g_cum[:, TSB - 1:TSB]))
        B.emit(DV, lambda e: e.tensor_scalar_max(g_t1[:], g_cum[:], -30.0),
               incs=[(s_dve, 1)])
        cd = s_dve.cnt
        B.emit(AC, lambda e: e.activation(g_gc[:], g_t1[:], AF.Exp),
               waits=[(s_dve, cd)], incs=[(s_act, 1)])
        ca2 = s_act.cnt
        B.emit(DV, lambda e: e.tensor_scalar_add(g_gc[:], g_gc[:], 1e-6),
               waits=[(s_act, ca2)])
        B.emit(DV, lambda e: e.reciprocal(g_a[:], g_gc[:]), incs=[(s_dve, 1)])
        c_recip = s_dve.cnt
        B.emit(DV, lambda e: e.tensor_mul(g_arr[:], g_a[:], g_r[:]),
               waits=[(s_dve, c_rsig), (s_dve, c_recip)])
        B.emit(DV, lambda e: e.tensor_scalar_mul(g_gcbf[:], g_gc[:], 1.0))
        B.emit(DV, lambda e: e.tensor_scalar_mul(g_arrbf[:], g_arr[:], 1.0),
               incs=[(s_dve, 1)])
        c_gatebf = s_dve.cnt

        k2_marks = {}
        for ft in range(NT):
            ph = pA[:, 512 * (ft % 2):512 * (ft % 2) + 512]
            if ft < 2:
                waits = [(s_act, k1_evacs[NT - 2 + ft])]
            else:
                waits = [(s_act, k2_marks[ft - 2])]
            for kt in range(NT):
                B.mm(lambda e, ph=ph, kt=kt, ft=ft: e.matmul(
                    ph, wk2_s[:, D * kt + 128 * ft: D * kt + 128 * ft + 128],
                    wsl(xnT, kt, LEAD, TSB), start=(kt == 0), stop=(kt == NT - 1)),
                    0, 0,
                    waits=waits if kt == 0 else [],
                    incs=[(s_pe, 1)] if kt == NT - 1 else [])
            ck = s_pe.cnt
            B.emit(AC, lambda e, ph=ph, ft=ft: e.activation(
                wsl(k2_s, ft, LEAD, TSB), ph, AF.Sigmoid,
                bias=bk2s_s[:, ft:ft + 1], scale=nf32(0.02 / 32.0)),
                waits=[(s_pe, ck)], incs=[(s_act, 1)])
            k2_marks[ft] = s_act.cnt
            if s == 0:
                B.dma(SY, s_dktd, k2d[128 * ft:128 * ft + 128, 0:TSB + 1],
                      wsl(k2_s, ft, LEAD - 1, TSB + 1),
                      waits=[(s_act, k2_marks[ft])])
            else:
                nk2 = min(TSB, L - (sb0 + 1))
                B.dma(SY, s_dktd, k2d[128 * ft:128 * ft + 128, sb0 + 1:sb0 + 1 + nk2],
                      wsl(k2_s, ft, LEAD, nk2), waits=[(s_act, k2_marks[ft])])
        c_k2 = s_act.cnt
        c_k2pe = s_pe.cnt

        for ft in range(NT):
            fp = ft % 2
            pX = pE if fp else pD
            bw = [(s_dve, c_gatebf)]
            if fp:
                bw += [(s_act, pE_free_act), (s_dve, pE_free_dve)]
            else:
                bw += [(s_act, pD_free_act), (s_dve, bc_dve_free[0])]
            B.mm(lambda e, ft=ft, pX=pX: e.matmul(
                pX[:, 0:TSB], eg_s[:, C * ft:C * ft + C], g_arrbf[:],
                start=True, stop=True),
                0, 0,
                waits=bw, incs=[(s_pe, 1)])
            cb = s_pe.cnt
            B.emit(AC, lambda e, fp=fp, pX=pX: e.copy(arrb_s[fp][:], pX[:, 0:TSB]),
                   waits=[(s_pe, cb)], incs=[(s_act, 1)])
            cab = s_act.cnt
            if fp:
                pE_free_act = cab
            else:
                pD_free_act = cab
            B.emit(DV, lambda e, ft=ft, fp=fp: e.tensor_mul(
                wsl(kt_s, ft, LEAD, TSB), wsl(kb_s, ft, LEAD, TSB), arrb_s[fp][:]),
                waits=[(s_act, cab)], incs=[(s_dve, 1)])
            ckt = s_dve.cnt
            if fp:
                pE_free_dve = ckt
            else:
                bc_dve_free[0] = ckt
            B.dma(SY, s_dktd, ktd[128 * ft:128 * ft + 128, sb0:sb0 + TSB],
                  wsl(kt_s, ft, LEAD, TSB), waits=[(s_dve, ckt)])
        qt_marks = {}
        for ft in range(NT):
            ph = pA[:, 512 * (ft % 2):512 * (ft % 2) + 512]
            if ft < 2:
                waits = [(s_act, k2_marks[NT - 2 + ft])]
            else:
                waits = [(s_dve, qt_marks[ft - 2][0]), (s_act, qt_marks[ft - 2][1])]
            for kt in range(NT):
                B.mm(lambda e, ph=ph, kt=kt, ft=ft: e.matmul(
                    ph, wq_s[:, D * kt + 128 * ft: D * kt + 128 * ft + 128],
                    wsl(xnT, kt, LEAD, TSB), start=(kt == 0), stop=(kt == NT - 1)),
                    0, 0,
                    waits=waits if kt == 0 else [],
                    incs=[(s_pe, 1)] if kt == NT - 1 else [])
            fp = ft % 2
            pX = pE if fp else pD
            bw = [(s_dve, c_gatebf)]
            if fp:
                bw += [(s_act, pE_free_act), (s_dve, pE_free_dve)]
            else:
                bw += [(s_act, pD_free_act), (s_dve, bc_dve_free[0])]
            B.mm(lambda e, ft=ft, pX=pX: e.matmul(
                pX[:, 0:TSB], eg_s[:, C * ft:C * ft + C], g_gcbf[:],
                start=True, stop=True),
                0, 0,
                waits=bw, incs=[(s_pe, 1)])
            cb = s_pe.cnt
            B.emit(AC, lambda e, fp=fp, pX=pX: e.copy(gcb_s[fp][:], pX[:, 0:TSB]),
                   waits=[(s_pe, cb)], incs=[(s_act, 1)])
            cgc = s_act.cnt
            if fp:
                pE_free_act = cgc
            else:
                pD_free_act = cgc
            cq = s_pe.cnt
            B.emit(DV, lambda e, ph=ph, ft=ft: e.tensor_scalar(
                q2scr[:], ph, bq_s[:, ft:ft + 1], -0.125,
                ALU.add, ALU.mult), waits=[(s_pe, cq)])
            B.emit(DV, lambda e, ft=ft: e.scalar_tensor_tensor(
                wsl(q2_s, ft, LEAD, TSB), q2scr[:], 0.02, q2scr[:],
                ALU.mult, ALU.max), incs=[(s_dve, 1)])
            cq2 = s_dve.cnt
            B.emit(DV, lambda e, ph=ph, ft=ft, fp=fp: e.scalar_tensor_tensor(
                wsl(qt_s, ft, LEAD, TSB), ph, bq_s[:, ft:ft + 1], gcb_s[fp][:],
                ALU.add, ALU.mult),
                waits=[(s_act, cgc)], incs=[(s_dve, 1)])
            if fp:
                pE_free_dve = s_dve.cnt
            else:
                bc_dve_free[0] = s_dve.cnt
            qt_marks[ft] = (s_dve.cnt, cgc)
        qt_all = qt_marks[NT - 1][0]

        ktd_cnt = s_dktd.cnt
        c_ktS = s_dve.cnt
        proj_pe_done[0] = s_pe.cnt
        early_ktok = {}
        for i0 in range(min(2, CPS)):
            c0 = CPS * s + i0
            p0 = c0 % 2
            ktw0 = [(s_dktd, ktd_cnt)]
            if c0 >= 2:
                ktw0.append((s_pe, ktokbuf_free[p0]))
            B.dma(AC, s_dktok[p0], ktok[p0][:], ktd[0:D, C * c0:C * c0 + C],
                  waits=ktw0, transpose=True)
            B.dma(AC, s_dktok[p0], k2tok[p0][:], k2d[0:D, C * c0:C * c0 + C],
                  transpose=True)
            early_ktok[c0] = s_dktok[p0].cnt
        chunk_hooks = {}
        if s + 1 < NSB:
            emit_ln_loads2(s + 1, 0)
            emit_ln_stats2(s + 1, 0)
            emit_ln_loads2(s + 1, 1)
            emit_ln_stats2(s + 1, 1)
            emit_ln_rstd_apply(s + 1)
            emit_ln_treads(s + 1)
        else:
            B.dma(GP, s_dw, wkt_s.ap().rearrange("p (k d) -> p k d", d=D),
                  wktf_d.rearrange("(k p) d -> p k d", p=128),
                  waits=[(s_pe, c_k1pe)])
            B.dma(GP, s_dw, wrt_s.ap().rearrange("p (k d) -> p k d", d=OF),
                  wrth_d.rearrange("(k p) d -> p k d", p=128),
                  waits=[(s_pe, c_k2pe)])
            B.dma(GP, s_dw, wvt_s.ap().rearrange("p (k d) -> p k d", d=OF),
                  wvth_d.rearrange("(k p) d -> p k d", p=128))
        if STOP <= 2:
            B.stopped = True

        for i in range(CPS):
            c = CPS * s + i
            par = c % 2
            if c in early_ktok:
                c_ktok = early_ktok[c]
            else:
                ktw = [(s_dktd, ktd_cnt)]
                if c >= 2:
                    ktw.append((s_pe, ktokbuf_free[par]))
                B.dma(AC, s_dktok[par], ktok[par][:], ktd[0:D, C * c:C * c + C],
                      waits=ktw, transpose=True)
                B.dma(AC, s_dktok[par], k2tok[par][:], k2d[0:D, C * c:C * c + C],
                      transpose=True)
                c_ktok = s_dktok[par].cnt
            if pend_od[0] is not None:
                pc, ppar, pcomb = pend_od[0]
                od_cnt[ppar] = B.dma(
                    SY, s_dod[ppar], od[C * pc + LEAD - 1:C * pc + LEAD - 1 + C, :],
                    ocomb[ppar][:], waits=[(s_dve, pcomb)])
                pend_od[0] = None
            c_k2z = 0
            if c == 0:
                B.emit(DV, lambda e: e.memset(k2tok[0][0:1, :], 0.0),
                       waits=[(s_dktok[par], c_ktok)], incs=[(s_dve, 1)])
                c_k2z = s_dve.cnt

            a0 = LEAD + C * i
            m1, m2 = a0 - 1, a0 - 2
            xt = xtok[s % 2]

            def kcut(nn, c=c):
                if c >= KCHUNK and nn >= KSUB:
                    B.stopped = True
            kcut(0)
            if c > 0:
                B.emit(AC, lambda e: e.copy(s2_sb[0:DH, 0:512], pC[0:DH, 0:512]),
                       waits=[(s_pe, ds2_mark[1])])
                B.emit(AC, lambda e: e.copy(s2_sb[DH:C, 512:1024], pC[DH:C, 512:1024]),
                       incs=[(s_act, 1)])
                sevac2_mark[0] = sevac2_mark[1] = s_act.cnt
            for g in range(2):
                hb = 8 * g

                def ps_block(par_jj, first_waits, inc_last):
                    for idx, jj in enumerate(par_jj):
                        j = hb + jj
                        ftj, rh = j // 2, DH * (j % 2)
                        B.mm(lambda e, jj=jj, ftj=ftj, rh=rh, a0=a0, m1=m1: e.matmul(
                            pA[:, 128 * jj:128 * jj + 128],
                            kt_s[rh:rh + DH, W * ftj + a0: W * ftj + a0 + C],
                            qt_s[rh:rh + DH, W * ftj + m1: W * ftj + m1 + C],
                            start=(jj % 4 == 0), stop=(jj % 4 == 3)),
                            rh, 0,
                            waits=first_waits if idx == 0 else [],
                            incs=[(s_pe, 1)] if (inc_last and idx == len(par_jj) - 1) else [])

                fw = []
                if c % CPS == 0 and g == 0:
                    fw = [(s_act, c_k2), (s_dve, qt_all), (s_dve, c_ktS)]
                fw.append((s_dve, pA_free_dve))
                ps_block((0, 2, 4, 6), fw, False)
                ps_block((1, 3, 5, 7), [], True)
                c_ps = s_pe.cnt
                kcut(2)

                def ds1_block(first_waits):
                    for idx, jj in enumerate((0, 2, 4, 6, 1, 3, 5, 7)):
                        j = hb + jj
                        rh2 = DH * (j % 2)
                        cl = 64 * (jj // 2) + 256 * (j % 2)
                        B.mm(lambda e, j=j, par=par, i=i, xt=xt, rh2=rh2,
                             cl=cl: e.matmul(
                            pD[rh2:rh2 + DH, cl:cl + DH],
                            ktok[par][:, DH * j:DH * j + DH],
                            xt[:, D * i + DH * j: D * i + DH * j + DH],
                            start=True, stop=True, skip_group_check=True),
                            0, rh2,
                            waits=first_waits if idx == 0 else [],
                            incs=[(s_pe, 1)] if idx == 7 else [])

                dw = [(s_dktok[par], c_ktok)]
                if c_k2z:
                    dw.append((s_dve, c_k2z))
                kcut(3)
                mb = (2 * c + g) % 2
                B.emit(DV, lambda e, mb=mb: e.tensor_mul(pms[mb][:, 0:512],
                                                         pA[:, 0:512], maskS[:]),
                       waits=[(s_pe, c_ps), (s_pe, pms_free[mb])])
                B.emit(DV, lambda e, mb=mb: e.tensor_mul(pms[mb][:, 512:1024],
                                                         pA[:, 512:1024], maskS[:]),
                       incs=[(s_dve, 1)])
                c_pm = s_dve.cnt
                kcut(4)

                def intra1_block():
                    for jj in range(8):
                        j = hb + jj
                        waits = []
                        if jj == 0:
                            waits = [(s_dve, c_pm), (s_act, pD_free_act),
                                     (s_dve, pD_free_dve)]
                        last = (jj == 7) and (c == 0)
                        B.mm(lambda e, jj=jj, j=j, i=i, xt=xt, last=last,
                             mb=mb: e.matmul(
                            pD[:, DH * jj:DH * jj + DH],
                            pms[mb][:, 128 * jj:128 * jj + 128],
                            xt[:, D * i + DH * j: D * i + DH * j + DH],
                            start=(jj == 0), stop=last),
                            0, 0,
                            waits=waits, incs=[(s_pe, 1)] if last else [])

                def inter1_block(par_jj, inc_last):
                    for idx, jj in enumerate(par_jj):
                        j = hb + jj
                        ftj, rh = j // 2, DH * (j % 2)
                        cl2 = 512 * (j % 2) + DH * (j // 2)
                        iw = [(s_dve, s1cast_mark[g])] if (idx == 0 and jj == 0) else []
                        B.mm(lambda e, jj=jj, ftj=ftj, rh=rh, m1=m1, cl2=cl2: e.matmul(
                            pD[:, DH * jj:DH * jj + DH],
                            qt_s[rh:rh + DH, W * ftj + m1: W * ftj + m1 + C],
                            s1_sb[rh:rh + DH, cl2:cl2 + DH],
                            start=False, stop=(jj == 7)),
                            rh, 0,
                            waits=iw,
                            incs=[(s_pe, 1)] if (inc_last and idx == len(par_jj) - 1) else [])

                def p2_block(par_jj, first_waits, inc_last):
                    for idx, jj in enumerate(par_jj):
                        j = hb + jj
                        ftj, rh = j // 2, DH * (j % 2)
                        B.mm(lambda e, jj=jj, ftj=ftj, rh=rh, m1=m1, m2=m2: e.matmul(
                            pB[:, 128 * jj:128 * jj + 128],
                            k2_s[rh:rh + DH, W * ftj + m1: W * ftj + m1 + C],
                            q2_s[rh:rh + DH, W * ftj + m2: W * ftj + m2 + C],
                            start=(jj % 4 == 0), stop=(jj % 4 == 3)),
                            rh, 0,
                            waits=first_waits if idx == 0 else [],
                            incs=[(s_pe, 1)] if (inc_last and idx == len(par_jj) - 1) else [])

                p2_block((0, 2, 4, 6), [(s_dve, pB_free_dve)], False)
                p2_block((1, 3, 5, 7), [], True)
                c_p2 = s_pe.cnt
                kcut(5)
                B.emit(DV, lambda e, mb=mb: e.tensor_mul(p2m[mb][:, 0:512],
                                                         pB[:, 0:512], maskS[:]),
                       waits=[(s_pe, c_p2), (s_pe, p2m_free[mb])])
                B.emit(DV, lambda e, mb=mb: e.tensor_mul(p2m[mb][:, 512:1024],
                                                         pB[:, 512:1024], maskS[:]),
                       incs=[(s_dve, 1)])
                c_p2m = s_dve.cnt
                pB_free_dve = c_p2m
                pA_free_dve = c_pm
                intra1_block()
                if c > 0:
                    inter1_block((0, 2, 4, 6), False)
                    inter1_block((1, 3, 5, 7), True)
                c_o1pe = s_pe.cnt
                pms_free[mb] = c_o1pe
                B.emit(AC, lambda e, g=g: e.copy(o1sf[:, 512 * g:512 * g + 512],
                                                 pD[:]),
                       waits=[(s_pe, c_o1pe), (s_pool, o1sf_free[g][0]),
                              (s_dve, o1sf_free[g][1])],
                       incs=[(s_act, 1)])
                c_o1ac = s_act.cnt
                pD_free_act = c_o1ac
                ew = [(s_act, c_o1ac)]
                if c > 0:
                    ew.append((s_pe, ds2_mark[g]))
                B.emit(PO, lambda e, g=g, i=i, xt=xt: e.tensor_sub(
                    e_buf[:, 512 * g:512 * g + 512],
                    xt[:, D * i + 512 * g: D * i + 512 * g + 512],
                    o1sf[:, 512 * g:512 * g + 512]),
                    waits=ew, incs=[(s_pool, 1)])
                c_e = s_pool.cnt
                kcut(6)
                ds1_block(dw + [(s_act, c_o1ac)])
                ds1_mark[g] = s_pe.cnt
                B.emit(DV, lambda e, g=g: e.tensor_add(
                    s1_acc[0:DH, 256 * g:256 * g + 256],
                    pD[0:DH, 0:256], s1_acc[0:DH, 256 * g:256 * g + 256]),
                    waits=[(s_pe, ds1_mark[g])])
                B.emit(DV, lambda e, g=g: e.tensor_add(
                    s1_acc[DH:C, 512 + 256 * g:768 + 256 * g],
                    pD[DH:C, 256:512],
                    s1_acc[DH:C, 512 + 256 * g:768 + 256 * g]),
                    incs=[(s_dve, 1)])
                pD_free_dve = s_dve.cnt
                B.emit(DV, lambda e, g=g: e.tensor_scalar_mul(
                    s1_sb[0:DH, 256 * g:256 * g + 256],
                    s1_acc[0:DH, 256 * g:256 * g + 256], 1.0))
                B.emit(DV, lambda e, g=g: e.tensor_scalar_mul(
                    s1_sb[DH:C, 512 + 256 * g:768 + 256 * g],
                    s1_acc[DH:C, 512 + 256 * g:768 + 256 * g], 1.0),
                    incs=[(s_dve, 1)])
                s1cast_mark[g] = s_dve.cnt
                kcut(7)
                for jj in range(8):
                    j = hb + jj
                    waits = []
                    if jj == 0:
                        waits = [(s_dve, c_p2m), (s_pool, c_e),
                                 (s_act, pE_free_act), (s_dve, pE_free_dve)]
                    last = (jj == 7) and (c == 0)
                    B.mm(lambda e, jj=jj, j=j, last=last, mb=mb: e.matmul(
                        pE[:, DH * jj:DH * jj + DH],
                        p2m[mb][:, 128 * jj:128 * jj + 128],
                        e_buf[:, DH * j:DH * j + DH],
                        start=(jj == 0), stop=last),
                        0, 0,
                        waits=waits, incs=[(s_pe, 1)] if last else [])

                def inter2_block(par_jj, first_waits, inc_last):
                    for idx, jj in enumerate(par_jj):
                        j = hb + jj
                        ftj, rh = j // 2, DH * (j % 2)
                        cl2 = 512 * (j % 2) + DH * (j // 2)
                        B.mm(lambda e, jj=jj, ftj=ftj, rh=rh, m2=m2, cl2=cl2: e.matmul(
                            pE[:, DH * jj:DH * jj + DH],
                            q2_s[rh:rh + DH, W * ftj + m2: W * ftj + m2 + C],
                            s2_sb[rh:rh + DH, cl2:cl2 + DH],
                            start=False, stop=(jj == 7)),
                            rh, 0,
                            waits=first_waits if idx == 0 else [],
                            incs=[(s_pe, 1)] if (inc_last and idx == len(par_jj) - 1) else [])

                def ds2_block(par_jj, first_waits, inc_last):
                    for idx, jj in enumerate(par_jj):
                        j = hb + jj
                        rh2, cl2 = DH * (j % 2), 512 * (j % 2) + DH * (j // 2)
                        st_, sp_ = (c == 0 and j <= 1), (c == NCH - 1 and j >= H - 2)
                        B.mm(lambda e, j=j, par=par, rh2=rh2, cl2=cl2, st_=st_,
                             sp_=sp_: e.matmul(
                            pC[rh2:rh2 + DH, cl2:cl2 + DH],
                            k2tok[par][:, DH * j:DH * j + DH],
                            e_buf[:, DH * j:DH * j + DH],
                            start=st_, stop=sp_, skip_group_check=True),
                            0, rh2,
                            waits=first_waits if idx == 0 else [],
                            incs=[(s_pe, 1)] if (inc_last and idx == len(par_jj) - 1) else [])

                dw2 = [(s_pool, c_e)]
                if c > 0:
                    dw2.append((s_act, sevac2_mark[g]))
                if c == 0:
                    c_o2pe = s_pe.cnt
                if c > 0:
                    inter2_block((0, 2, 4, 6), [], False)
                ds2_block((0, 2, 4, 6), dw2, False)
                if c > 0:
                    inter2_block((1, 3, 5, 7), [], True)
                    c_o2pe = s_pe.cnt
                p2m_free[mb] = c_o2pe
                kcut(8)
                ds2_block((1, 3, 5, 7), [], True)
                ds2_mark[g] = s_pe.cnt
                kcut(9)
                if g == 1:
                    ktokbuf_free[par] = s_pe.cnt
                cw = [(s_pe, c_o2pe), (s_act, c_o1ac), (s_dod[par], od_cnt[par])]
                B.emit(DV, lambda e, g=g, par=par: e.tensor_sub(
                    ocomb[par][:, 512 * g:512 * g + 512],
                    o1sf[:, 512 * g:512 * g + 512], pE[:]),
                    waits=cw, incs=[(s_dve, 1)])
                c_comb = s_dve.cnt
                pE_free_dve = c_comb
                o1sf_free[g] = (c_e, c_comb)
            pend_od[0] = (c, par, c_comb)
            if i in chunk_hooks:
                chunk_hooks[i]()
        rec_pe_done = s_pe.cnt
        if STOP <= 3 and s == 0:
            B.stopped = True
        if s < NSB - 1:
            for t in (qt_s, q2_s, k2_s):
                B.emit(PO, lambda e, t=t: e.tensor_copy(
                    t.ap().rearrange("p (b w) -> p b w", w=W)[:, :, 0:LEAD],
                    t.ap().rearrange("p (b w) -> p b w", w=W)[:, :, TSB:TSB + LEAD]),
                    waits=[(s_pe, rec_pe_done)], incs=[(s_pool, 1)])
            c_lead = s_pool.cnt
            pA_free_dve = max(pA_free_dve, 0)
            B.emit(PE, lambda e: e.nop(), waits=[(s_pool, c_lead)])

    if STOP <= 4:
        B.stopped = True
    if pend_od[0] is not None:
        pc, ppar, pcomb = pend_od[0]
        od_cnt[ppar] = B.dma(
            SY, s_dod[ppar], od[C * pc + LEAD - 1:C * pc + LEAD - 1 + C, :],
            ocomb[ppar][:], waits=[(s_dve, pcomb)])
        pend_od[0] = None
    for g in range(2):
        B.emit(AC, lambda e, g=g: e.copy(
            s2_sb[0:DH, 256 * g:256 * g + 256], pC[0:DH, 256 * g:256 * g + 256]),
            waits=[(s_pe, ds2_mark[g])])
        B.emit(AC, lambda e, g=g: e.copy(
            s2_sb[DH:C, 512 + 256 * g:768 + 256 * g],
            pC[DH:C, 512 + 256 * g:768 + 256 * g]), incs=[(s_act, 1)])
    c_sfin = s_act.cnt
    for idx, j in enumerate([jv for jv in range(0, H, 2)] + [jv for jv in range(1, H, 2)]):
        ftj, rh = j // 2, DH * (j % 2)
        waits = [(s_act, c_sfin), (s_dve, pA_free_dve),
                 (s_dve, s1cast_mark[0]), (s_dve, s1cast_mark[1])] if idx == 0 else []
        cl2 = 512 * (j % 2) + DH * (j // 2)
        B.mm(lambda e, j=j, ftj=ftj, rh=rh, cl2=cl2: e.matmul(
            pA[0:1, DH * j:DH * j + DH],
            qt_s[rh:rh + DH, W * ftj + LEAD + 511: W * ftj + LEAD + 512],
            s1_sb[rh:rh + DH, cl2:cl2 + DH], start=(j % 8 == 0), stop=(j % 8 == 7),
            skip_group_check=True),
            rh, 0,
            waits=waits)
    for idx, j in enumerate([jv for jv in range(0, H, 2)] + [jv for jv in range(1, H, 2)]):
        ftj, rh = j // 2, DH * (j % 2)
        waits = [(s_dve, pE_free_dve), (s_act, pE_free_act),
                 (s_dve, pB_free_dve), (s_act, c_sfin)] if idx == 0 else []
        cl2 = 512 * (j % 2) + DH * (j // 2)
        B.mm(lambda e, j=j, ftj=ftj, rh=rh, cl2=cl2: e.matmul(
            pB[0:1, DH * j:DH * j + DH],
            q2_s[rh:rh + DH, W * ftj + LEAD + 510: W * ftj + LEAD + 511],
            s2_sb[rh:rh + DH, cl2:cl2 + DH], start=(j % 8 == 0), stop=(j % 8 == 7),
            skip_group_check=True),
            rh, 0,
            waits=waits, incs=[(s_pe, 1)] if idx == H - 1 else [])
    c_mini = s_pe.cnt
    B.emit(AC, lambda e: e.copy(minis[:], pA[0:1, 0:1024]),
           waits=[(s_pe, c_mini)], incs=[(s_act, 1)])
    c_mac = s_act.cnt
    B.emit(DV, lambda e: e.tensor_sub(ocomb[0][0:1, :], minis[:], pB[0:1, 0:1024]),
           waits=[(s_act, c_mac), (s_dod[0], s_dod[0].cnt)], incs=[(s_dve, 1)])
    c_mdv = s_dve.cnt
    B.dma(SY, s_dod[0], od[L + LEAD - 1:L + LEAD, :], ocomb[0][0:1, :],
          waits=[(s_dve, c_mdv)])
    c_od_all = (s_dod[0].cnt, s_dod[1].cnt)

    if STOP <= 5:
        B.stopped = True
    c_wffn = s_dw.cnt
    for kt in range(NT):
        B.emit(DV, lambda e, kt=kt: e.tensor_scalar_mul(
            wrA_s[:, OF * kt:OF * kt + OF], wrt_s[:, OF * kt:OF * kt + OF],
            mur_s[:, kt:kt + 1]),
            waits=[(s_dw, c_wffn)] if kt == 0 else [])
        B.emit(DV, lambda e, kt=kt: e.tensor_scalar_mul(
            wrB_s[:, OF * kt:OF * kt + OF], wrt_s[:, OF * kt:OF * kt + OF],
            omr_s[:, kt:kt + 1]),
            incs=[(s_dve, 1)] if kt == NT - 1 else [])
    c_wfold = s_dve.cnt

    ffn = dict(fread={}, fread_ft={}, cmix={}, kproj={}, cka={},
               rproj={}, vproj={}, sproj={},
               out_st={0: 0, 1: 0}, out_dve={0: 0, 1: 0},
               prev_r={0: 0, 1: 0}, prev_v={0: 0, 1: 0}, prev_s={0: 0, 1: 0})

    def ffn_head(q):
        b = q % 2
        tb = TF * q
        if q == 0:
            fw = [(s_dxnd, s_dxnd.cnt), (s_pool, s_pool.cnt),
                  (s_pe, c_mini), (s_act, c_mac), (s_dve, c_mdv)]
            ow = [(s_dod[0], c_od_all[0]), (s_dod[1], c_od_all[1]),
                  (s_pe, c_mini), (s_act, c_mac), (s_dve, c_mdv)]
        else:
            fw = [(s_pe, ffn["rproj"][q - 2])] if q >= 2 else []
            ow = [(s_pe, ffn["sproj"][q - 2])] if q >= 2 else []
        fts = []
        for half in range(2):
            for i in range(4):
                ft = 4 * half + i
                c = B.dma(SY, s_dxn[b][half], xnF_b[b][:, WF * ft:WF * ft + WF],
                          xnd[tb:tb + WF, 128 * ft:128 * ft + 128],
                          waits=fw if ft == 0 else [], transpose=True)
            fts.append((s_dxn[b][half], c))
        ffn["fread_ft"][q] = fts
        for ft in range(NT):
            B.dma(SY, s_dot[b], ot_s[b][:, TF * ft:TF * ft + TF],
                  od[tb + LEAD:tb + LEAD + TF, 128 * ft:128 * ft + 128],
                  waits=ow if ft == 0 else [], transpose=True)
        ffn["fread"][q] = (s_dot[b], s_dot[b].cnt)

    def ffn_mix(q):
        b = q % 2
        xnq = xnF_b[b]
        mu_c, om_c, dst = muk_s, omk_s, xk_s
        war = (s_pe, ffn["kproj"][q - 1]) if q >= 1 else None
        for ft in range(NT):
            fwm = [ffn["fread_ft"][q][ft // 4]] if ft % 4 == 0 else []
            if ft == 0 and war:
                fwm.append(war)
            B.emit(DV, lambda e, ft=ft, om_c=om_c, xnq=xnq: e.tensor_scalar_mul(
                kr2_s[:], xnq[:, WF * ft + LEAD - 1: WF * ft + LEAD - 1 + TF],
                om_c[:, ft:ft + 1]), waits=fwm)
            B.emit(DV, lambda e, ft=ft, mu_c=mu_c, dst=dst, xnq=xnq: e.scalar_tensor_tensor(
                dst[:, TF * ft:TF * ft + TF],
                xnq[:, WF * ft + LEAD: WF * ft + LEAD + TF],
                mu_c[:, ft:ft + 1], kr2_s[:], ALU.mult, ALU.add),
                incs=[(s_dve, 1)])
        ffn["cmix"][q] = s_dve.cnt

    def ffn_kproj(q):
        kevac = {}
        for m in range(NT):
            ph = pA[:, 512 * (m % 2):512 * (m % 2) + 512]
            if m < 2:
                waits = [(s_dve, ffn["cmix"][q]), (s_dw, c_wffn)]
                if q == 0:
                    waits += [(s_dve, pA_free_dve), (s_act, c_mac),
                              (s_dve, c_mdv)]
                else:
                    waits.append((s_dve, ffn["cka"][q - 1]))
            else:
                waits = [(s_dve, kevac[m - 2])]
            for kt in range(NT):
                B.mm(lambda e, ph=ph, kt=kt, m=m: e.matmul(
                    ph, wkt_s[:, D * kt + 128 * m: D * kt + 128 * m + 128],
                    xk_s[:, TF * kt: TF * kt + TF],
                    start=(kt == 0), stop=(kt == NT - 1)),
                    0, 0,
                    waits=waits if kt == 0 else [],
                    incs=[(s_pe, 1)] if kt == NT - 1 else [])
            ckp = s_pe.cnt
            krw = [(s_pe, ckp)]
            if m >= 1:
                krw.append((s_dve, kevac[m - 1]))
            B.emit(AC, lambda e, ph=ph: e.activation(
                kr_s[:], ph, AF.Relu),
                waits=krw, incs=[(s_act, 1)])
            cka = s_act.cnt
            kaw = [(s_act, cka)]
            if m == 0 and q >= 1:
                kaw.append((s_pe, ffn["vproj"][q - 1]))
            B.emit(DV, lambda e, m=m: e.tensor_mul(
                ka_s[:, TF * m: TF * m + TF],
                kr_s[:], kr_s[:]),
                waits=kaw, incs=[(s_dve, 1)])
            kevac[m] = s_dve.cnt
        ffn["cka"][q] = s_dve.cnt
        ffn["kproj"][q] = s_pe.cnt

    def ffn_tiles(q, hooks=None):
        b = q % 2
        tb = TF * q
        prev_r, prev_v, prev_s = ffn["prev_r"], ffn["prev_v"], ffn["prev_s"]
        for ttl in range(4):
            if hooks and ttl in hooks:
                hooks[ttl]()
            pr = ttl % 2
            ob = ttl % 2
            xb = x_ln[ob]
            cx = B.dma(GP, s_dxf[ob], xb[:, 0:OF],
                       xh_d[tb + 128 * ttl:tb + 128 * ttl + 128, :],
                       waits=[(s_dve, ffn["out_dve"][ob] or ln_free[ob])])
            waits0 = [(s_dve, c_wfold)]
            if ttl == 0:
                waits0 += list(ffn["fread_ft"][q])
            if prev_r[pr]:
                waits0.append((s_act, prev_r[pr]))
            else:
                waits0 += [(s_dve, c_mdv), (s_act, c_sfin)]
            for half in range(2):
                wsrc = wrA_s if half == 0 else wrB_s
                sh = LEAD if half == 0 else LEAD - 1
                for kt in range(NT):
                    st_ = (half == 0 and kt == 0)
                    sp_ = (half == 1 and kt == NT - 1)
                    B.mm(lambda e, kt=kt, ttl=ttl, pr=pr, b=b, wsrc=wsrc,
                         sh=sh, st_=st_, sp_=sp_: e.matmul(
                        pB[:, 512 * pr:512 * pr + 512],
                        xnF_b[b][:, WF * kt + sh + 128 * ttl: WF * kt + sh + 128 * ttl + 128],
                        wsrc[:, OF * kt: OF * kt + OF],
                        start=st_, stop=sp_),
                        0, 0,
                        waits=waits0 if st_ else [],
                        incs=[(s_pe, 1)] if sp_ else [])
            crp = s_pe.cnt
            if ttl >= 2:
                ffn["rproj"][q] = crp
            B.emit(AC, lambda e, pr=pr: e.activation(
                r_bf[pr][:], pB[:, 512 * pr:512 * pr + 512], AF.Sigmoid),
                   waits=[(s_pe, crp)], incs=[(s_act, 1)])
            prev_r[pr] = s_act.cnt
            waits0 = [(s_dve, ffn["cka"][q])]
            if prev_v[pr]:
                waits0.append((s_dve, prev_v[pr]))
            elif q == 0:
                waits0.append((s_act, c_sfin))
            for kt in range(NT):
                B.mm(lambda e, kt=kt, ttl=ttl, pr=pr: e.matmul(
                    pC[:, 512 * pr:512 * pr + 512],
                    ka_s[:, TF * kt + 128 * ttl: TF * kt + 128 * ttl + 128],
                    wvt_s[:, OF * kt: OF * kt + OF],
                    start=(kt == 0), stop=(kt == NT - 1)),
                    0, 0,
                    waits=waits0 if kt == 0 else [],
                    incs=[(s_pe, 1)] if kt == NT - 1 else [])
            cvp = s_pe.cnt
            if ttl >= 2:
                ffn["vproj"][q] = cvp
            ps_t = pD if pr == 0 else pE
            waits0 = [ffn["fread"][q], (s_dw, c_winit)]
            if prev_s[pr]:
                waits0.append((s_dve, prev_s[pr]))
            else:
                waits0 += [(s_act, c_mac), (s_dve, c_mdv), (s_act, pD_free_act),
                           (s_dve, pE_free_dve)]
            for kt in range(NT):
                B.mm(lambda e, kt=kt, ttl=ttl, ps_t=ps_t, b=b: e.matmul(
                    ps_t[:, 0:512],
                    ot_s[b][:, TF * kt + 128 * ttl: TF * kt + 128 * ttl + 128],
                    wp_s[:, OF * kt: OF * kt + OF],
                    start=(kt == 0), stop=False),
                    0, 0,
                    waits=waits0 if kt == 0 else [])
            B.mm(lambda e, ps_t=ps_t: e.matmul(
                ps_t[:, 0:512], ones2_s[:], bp2[:], start=False, stop=True),
                0, 0,
                incs=[(s_pe, 1)])
            csp = s_pe.cnt
            if ttl >= 2:
                ffn["sproj"][q] = csp
            B.emit(DV, lambda e, pr=pr: e.tensor_mul(
                gg1[pr][:], r_bf[pr][:], pC[:, 512 * pr:512 * pr + 512]),
                   waits=[(s_pe, cvp), (s_act, prev_r[pr]),
                          (s_dout[pr], ffn["out_st"][pr])], incs=[(s_dve, 1)])
            prev_v[pr] = s_dve.cnt
            cg1 = s_dve.cnt
            B.emit(AC, lambda e, pr=pr: e.activation(gg2[pr][:], gg1[pr][:],
                                                     AF.Sigmoid),
                   waits=[(s_dve, cg1)], incs=[(s_act, 1)])
            cg2 = s_act.cnt
            B.emit(DV, lambda e, pr=pr: e.tensor_mul(gg2[pr][:], gg2[pr][:],
                                                     gg1[pr][:]),
                   waits=[(s_act, cg2)])
            B.emit(DV, lambda e, pr=pr, ps_t=ps_t: e.tensor_mul(
                gg1[pr][:], ps_t[:, 0:512], gg2[pr][:]),
                   waits=[(s_pe, csp)], incs=[(s_dve, 1)])
            prev_s[pr] = s_dve.cnt
            B.emit(DV, lambda e, xb=xb, pr=pr, ob=ob, cx=cx: e.tensor_add(
                gg1[pr][:], gg1[pr][:], xb[:, 0:OF]),
                waits=[(s_dxf[ob], cx)],
                incs=[(s_dve, 1)])
            cfin = s_dve.cnt
            ffn["out_dve"][ob] = cfin
            ffn["out_st"][ob] = B.dma(
                GP, s_dout[ob], out_d[tb + 128 * ttl:tb + 128 * ttl + 128, :],
                gg1[pr][:], waits=[(s_dve, cfin)])

    ffn_head(0)
    ffn_mix(0)
    for q in range(NSB):
        if q + 1 < NSB:
            ffn_head(q + 1)
        ffn_kproj(q)
        if q + 1 < NSB:
            ffn_tiles(q, hooks={1: lambda q=q: ffn_mix(q + 1)})
        else:
            ffn_tiles(q)
    B.stopped = False
    _alls = [s_dc, s_dw, s_dxnd, s_dxnt, s_dktd] + s_dot + s_dx + s_dxf + s_dxn[0] + s_dxn[1] + s_dktok + s_dod + s_dout
    B.emit(SY, lambda e: e.nop(),
           waits=[(sm, sm.cnt) for sm in _alls])
    B.emit(GP, lambda e: e.engine_nop(),
           waits=[(s_dw, s_dw.cnt)])

    semctx = []
    for s in B.sems.values():
        cm = nc.semaphore(s.name)
        semctx.append(cm)
        s.h = cm.__enter__()
    with nc.Block() as block:
        def run(eng_name):
            names = ("gpsimd", "pool") if eng_name == "gpsimd" else (eng_name,)
            use_drain = KDRAIN and eng_name in ("act", "dve", "gpsimd")
            def body(e):
                first = True
                for eng, waits, fn, incs in B.steps:
                    if eng not in names:
                        continue
                    for sem, val in waits:
                        e.wait_ge(sem.h, val)
                    if use_drain and not first:
                        e.drain()
                    first = False
                    inst = fn(e)
                    for sem, d in incs:
                        inst.then_inc(sem.h, d)
            return body

        block.tensor(run("pe"))
        block.scalar(run("act"))
        block.vector(run("dve"))
        block.gpsimd(run("gpsimd"))
        block.sync(run("sync"))

    return nc


_NC_CACHE = {}


def kernel(**inputs):
    ins = {k: np.asarray(v) for k, v in inputs.items()}
    x = ins["L0_attn_x"].astype(np.float32)
    Wq = np.ascontiguousarray(ins["L0_attn_Wq"], np.float32)
    bq = np.ascontiguousarray(ins["L0_attn_bq"], np.float32)
    Wk1 = np.ascontiguousarray(ins["L0_attn_Wk1"], np.float32)
    bk1 = np.ascontiguousarray(ins["L0_attn_bk1"], np.float32)
    Wk2 = np.ascontiguousarray(ins["L0_attn_Wk2"], np.float32)
    bk2 = np.ascontiguousarray(ins["L0_attn_bk2"], np.float32)
    Wg = np.ascontiguousarray(ins["L0_attn_Wg"], np.float32)
    bg = np.ascontiguousarray(ins["L0_attn_bg"], np.float32)
    Ws = np.ascontiguousarray(ins["L0_attn_Ws"], np.float32)
    bs = np.ascontiguousarray(ins["L0_attn_bs"], np.float32)
    Wp = np.ascontiguousarray(ins["L0_attn_Wp"], np.float32)
    bp = np.ascontiguousarray(ins["L0_attn_bp"], np.float32)
    WrT = np.ascontiguousarray(ins["L0_ffn_W_r"].T, np.float32)
    WkT = np.ascontiguousarray(ins["L0_ffn_W_k"].T, np.float32)
    WvT = np.ascontiguousarray(ins["L0_ffn_W_v"].T, np.float32)
    mu_r = np.ascontiguousarray(ins["L0_ffn_mu_r"], np.float32)
    mu_k = np.ascontiguousarray(ins["L0_ffn_mu_k"], np.float32)

    wgb = np.zeros((D, H), np.float32)
    wsb = np.zeros((D, H), np.float32)
    for j in range(H):
        wgb[64 * j:64 * j + 64, j] = Wg[:, 0]
        wsb[64 * j:64 * j + 64, j] = Ws[:, 0]
    bgv = np.repeat(bg[0], H).reshape(H, 1).astype(np.float32)
    bsv = np.repeat(bs[0], H).reshape(H, 1).astype(np.float32)
    maskS = np.tile(np.triu(np.ones((C, C), np.float32), 1), (1, 4)).astype(ml_dtypes.bfloat16)
    EG = np.zeros((H, NT * C), np.float32)
    for j in range(H):
        EG[j, C * (j // 2) + 64 * (j % 2): C * (j // 2) + 64 * (j % 2) + 64] = 1.0
    EG = EG.astype(ml_dtypes.bfloat16)
    ones2 = np.zeros((2, C), np.float32)
    ones2[0, :] = 1.0
    ones2 = ones2.astype(ml_dtypes.bfloat16)

    nc = build_nc()

    in_maps = []
    for c in range(8):
        b, p = c // 2, c % 2
        sl = slice(OF * p, OF * p + OF)
        m = dict(
            x=np.ascontiguousarray(x[b]),
            xh=np.ascontiguousarray(x[b][:, sl]),
            Wq=Wq, Wk1=Wk1, Wk2=Wk2,
            WpH=np.ascontiguousarray(Wp[:, sl]),
            WrTH=np.ascontiguousarray(WrT[:, sl]),
            WkT=WkT,
            WvTH=np.ascontiguousarray(WvT[:, sl]),
            bq=bq, bk1=bk1, bk2=bk2,
            bpH=np.ascontiguousarray(bp[sl]),
            mu_r=mu_r, mu_k=mu_k, wgb=wgb, wsb=wsb, bgv=bgv, bsv=bsv,
            maskS=maskS, EGF=EG, ones2=ones2,
        )
        in_maps.append(m)

    res = run_bass_kernel_spmd(nc, in_maps, core_ids=list(range(8)))
    out = np.empty((4, L, D), np.float32)
    for c in range(8):
        b, p = c // 2, c % 2
        out[b, :, OF * p:OF * p + OF] = np.asarray(res.results[c]["out"])
    return out


# version 35 (final)
# speedup vs baseline: 1.0402x; 1.0402x over previous
import os
import numpy as np
import ml_dtypes
import concourse.bass as bass
import concourse.mybir as mybir
from concourse.bass_utils import run_bass_kernel_spmd

F32 = mybir.dt.float32
BF16 = mybir.dt.bfloat16
AF = mybir.ActivationFunctionType
ALU = mybir.AluOpType

L, D, H, DH = 4096, 1024, 16, 64
C = 128
TSB = 512
NSB = L // TSB
CPS = TSB // C
NCH = L // C
LEAD = 16
W = TSB + LEAD
NT = 8
OF = 512
TF = 512
WF = TF + LEAD
nf32 = np.float32


class Sem:
    def __init__(self, name):
        self.name, self.cnt, self.h = name, 0, None


STOP = int(os.environ.get("KSTOP", "99"))
KCHUNK = int(os.environ.get("KCHUNK", "99"))
KSUB = float(os.environ.get("KSUB", "99"))
KDRAIN = int(os.environ.get("KDRAIN", "1"))


class Builder:
    def __init__(self):
        self.steps = []
        self.sems = {}
        self.stopped = False

    def sem(self, name):
        s = Sem(name)
        self.sems[name] = s
        return s

    def emit(self, eng, fn, waits=(), incs=()):
        if self.stopped:
            return
        for s, d in incs:
            s.cnt += d
        self.steps.append((eng, [(s, v) for s, v in waits if v > 0], fn, list(incs)))

    def mm(self, fn, sbase, obase, waits=(), incs=()):
        if self.stopped:
            return
        if (sbase, obase) != getattr(self, "pe_base", (0, 0)):
            self.steps.append(("pe", [], lambda e: e.drain(), []))
            self.pe_base = (sbase, obase)
        self.emit("pe", fn, waits, incs)

    def dma(self, queue, sem, out, in_, waits=(), transpose=False):
        if transpose:
            fn = lambda e, o=out, i=in_: e.dma_start_transpose(o, i)
        else:
            fn = lambda e, o=out, i=in_: e.dma_start(out=o, in_=i)
        self.emit(queue, fn, waits=waits, incs=[(sem, 16)])
        return sem.cnt


def build_nc(stop=99):
    nc = bass.Bass("TRN2", target_bir_lowering=False)
    B = Builder()

    x_d = nc.dram_tensor("x", [L, D], F32, kind="ExternalInput").ap()
    xh_d = nc.dram_tensor("xh", [L, OF], F32, kind="ExternalInput").ap()
    wq_d = nc.dram_tensor("Wq", [D, D], F32, kind="ExternalInput").ap()
    wk1_d = nc.dram_tensor("Wk1", [D, D], F32, kind="ExternalInput").ap()
    wk2_d = nc.dram_tensor("Wk2", [D, D], F32, kind="ExternalInput").ap()
    wph_d = nc.dram_tensor("WpH", [D, OF], F32, kind="ExternalInput").ap()
    wrth_d = nc.dram_tensor("WrTH", [D, OF], F32, kind="ExternalInput").ap()
    wktf_d = nc.dram_tensor("WkT", [D, D], F32, kind="ExternalInput").ap()
    wvth_d = nc.dram_tensor("WvTH", [D, OF], F32, kind="ExternalInput").ap()
    bq_d = nc.dram_tensor("bq", [D], F32, kind="ExternalInput").ap()
    bk1_d = nc.dram_tensor("bk1", [D], F32, kind="ExternalInput").ap()
    bk2_d = nc.dram_tensor("bk2", [D], F32, kind="ExternalInput").ap()
    bph_d = nc.dram_tensor("bpH", [OF], F32, kind="ExternalInput").ap()
    mur_d = nc.dram_tensor("mu_r", [D], F32, kind="ExternalInput").ap()
    muk_d = nc.dram_tensor("mu_k", [D], F32, kind="ExternalInput").ap()
    wgb_d = nc.dram_tensor("wgb", [D, H], F32, kind="ExternalInput").ap()
    wsb_d = nc.dram_tensor("wsb", [D, H], F32, kind="ExternalInput").ap()
    bgv_d = nc.dram_tensor("bgv", [H, 1], F32, kind="ExternalInput").ap()
    bsv_d = nc.dram_tensor("bsv", [H, 1], F32, kind="ExternalInput").ap()
    mask_d = nc.dram_tensor("maskS", [C, 512], BF16, kind="ExternalInput").ap()
    eg_d = nc.dram_tensor("EGF", [H, NT * C], BF16, kind="ExternalInput").ap()
    ones2_d = nc.dram_tensor("ones2", [2, C], BF16, kind="ExternalInput").ap()
    out_d = nc.dram_tensor("out", [L, OF], F32, kind="ExternalOutput").ap()

    xnd = nc.dram_tensor("xnd", [L + LEAD, D], BF16).ap()
    ktd = nc.dram_tensor("ktd", [D, L], BF16).ap()
    k2d = nc.dram_tensor("k2d", [D, L], BF16).ap()
    od = nc.dram_tensor("Od", [L + LEAD, D], BF16).ap()

    ARENA = 200 * 1024
    arena_h = nc.alloc_sbuf_tensor("myarena", [128, ARENA], mybir.dt.uint8)
    abase = nc.lookup_mloc(arena_h).addr
    off = [abase]

    def alloc(name, shape, dt):
        per = int(np.prod(shape[1:])) * mybir.dt.size(dt)
        a = (off[0] + 31) // 32 * 32
        hdl = nc.alloc_sbuf_tensor_at(name, list(shape), dt, offset=a)
        off[0] = a + per
        return hdl

    maskS = alloc("maskS_s", [C, 512], BF16)
    eg_s = alloc("eg_s", [H, NT * C], BF16)
    ones2_s = alloc("ones2_s", [2, C], BF16)
    wgb_s = alloc("wgb_s", [C, H * NT], BF16)
    wsb_s = alloc("wsb_s", [C, H * NT], BF16)
    bq_s = alloc("bq_s", [C, NT], F32)
    bk1_s = alloc("bk1_s", [C, NT], F32)
    bk2s_s = alloc("bk2s_s", [C, NT], F32)
    nbq8_s = alloc("nbq8_s", [C, NT], F32)
    mur_s = alloc("mur_s", [C, NT], F32)
    muk_s = alloc("muk_s", [C, NT], F32)
    omr_s = alloc("omr_s", [C, NT], F32)
    omk_s = alloc("omk_s", [C, NT], F32)
    bgv_s = alloc("bgv_s", [H, 1], F32)
    bsv_s = alloc("bsv_s", [H, 1], F32)
    carry = alloc("carry", [H, 1], F32)
    zrow = alloc("zrow", [LEAD, D], BF16)
    bp2 = alloc("bp2", [2, OF], BF16)
    ones16 = alloc("ones16", [H, TSB], F32)
    g_t1 = alloc("g_t1", [H, TSB], F32)
    g_t2 = alloc("g_t2", [H, TSB], F32)
    g_cum = alloc("g_cum", [H, TSB], F32)
    g_a = alloc("g_a", [H, TSB], F32)
    g_r = alloc("g_r", [H, TSB], F32)
    g_gc = g_cum
    g_arr = g_t2
    g_gcbf = alloc("g_gcbf", [H, TSB], BF16)
    g_arrbf = alloc("g_arrbf", [H, TSB], BF16)
    warena = off[0]
    wq_s = alloc("wq_s", [C, NT * D], BF16)
    wk1a = off[0]
    wk1_s = alloc("wk1_s", [C, NT * D], BF16)
    wk2a = off[0]
    wk2_s = alloc("wk2_s", [C, NT * D], BF16)
    wtop = off[0]
    off[0] = wk1a
    wkt_s = alloc("wkt_s", [C, NT * D], BF16)
    off[0] = wk2a
    wrt_s = alloc("wrt_s", [C, NT * OF], BF16)
    wvt_s = alloc("wvt_s", [C, NT * OF], BF16)
    off[0] = warena
    wrA_s = alloc("wrA_s", [C, NT * OF], BF16)
    wrB_s = alloc("wrB_s", [C, NT * OF], BF16)
    off[0] = max(off[0], wtop)
    wp_s = alloc("wp_s", [C, NT * OF], BF16)
    ln1 = alloc("ln1", [C, D], BF16)
    ln2 = alloc("ln2", [C, D], BF16)
    stat_m = alloc("stat_m", [C, CPS], F32)
    stat_s1 = alloc("stat_s1", [C, CPS], F32)
    stat_s2 = alloc("stat_s2", [C, CPS], F32)
    stat_v = alloc("stat_v", [C, CPS], F32)
    stat_r = alloc("stat_r", [C, CPS], F32)
    x_ln = [alloc("x_ln0", [C, D], F32), alloc("x_ln1", [C, D], F32)]
    gcb_s = [alloc("gcb_s0", [C, TSB], BF16), alloc("gcb_s1", [C, TSB], BF16)]
    q2scr = alloc("q2scr", [C, TSB], BF16)
    arrb_s = [alloc("arrb_s0", [C, TSB], BF16), alloc("arrb_s1", [C, TSB], BF16)]
    ktok = [alloc("ktok0", [C, D], BF16), alloc("ktok1", [C, D], BF16)]
    k2tok = [alloc("k2tok0", [C, D], BF16), alloc("k2tok1", [C, D], BF16)]
    e_buf = alloc("e_buf", [C, D], BF16)
    o1sf = alloc("o1sf", [C, D], F32)
    ocomb = [alloc("ocomb0", [C, D], BF16), alloc("ocomb1", [C, D], BF16)]
    pms = [alloc("pms0", [C, D], BF16), alloc("pms1", [C, D], BF16)]
    p2m = [alloc("p2m0", [C, D], BF16), alloc("p2m1", [C, D], BF16)]
    minis = alloc("minis", [1, D], F32)
    s1_sb = alloc("s1_sb", [C, 1024], BF16)
    s2_sb = alloc("s2_sb", [C, 1024], BF16)
    arena = off[0]
    xnT = alloc("xnT", [C, NT * W], BF16)
    qt_s = alloc("qt_s", [C, NT * W], BF16)
    kb_s = alloc("kb_s", [C, NT * W], BF16)
    kt_s = alloc("kt_s", [C, NT * W], BF16)
    k2_s = alloc("k2_s", [C, NT * W], BF16)
    q2_s = alloc("q2_s", [C, NT * W], BF16)
    xtok = [alloc("xtok0", [C, CPS * D], BF16), alloc("xtok1", [C, CPS * D], BF16)]
    top = off[0]
    off[0] = arena
    xnF_b = [alloc("xnF0", [C, NT * WF], BF16),
             alloc("xnF1", [C, NT * WF], BF16)]
    xk_s = alloc("xk_s", [C, NT * TF], BF16)
    ka_s = alloc("ka_s", [C, NT * TF], BF16)
    ot_s = [alloc("ot_s0", [C, NT * TF], BF16), alloc("ot_s1", [C, NT * TF], BF16)]
    kr_s = alloc("kr_s", [C, TF], BF16)
    kr2_s = alloc("kr2_s", [C, TF], BF16)
    r_bf = [alloc("r_bf0", [C, OF], BF16), alloc("r_bf1", [C, OF], BF16)]
    gg1 = [alloc("gg1a", [C, OF], F32), alloc("gg1b", [C, OF], F32)]
    gg2 = [alloc("gg2a", [C, OF], F32), alloc("gg2b", [C, OF], F32)]
    off[0] = max(off[0], top)
    s1_acc = alloc("s1_acc", [C, 1024], F32)
    assert off[0] - abase <= ARENA, f"SBUF over budget: {off[0] - abase}"

    ctxs = []

    def penter(cm):
        ctxs.append(cm)
        return cm.__enter__()

    pA = penter(nc.psum_tensor("pA", [C, 1024], F32))
    pB = penter(nc.psum_tensor("pB", [C, 1024], F32))
    pC = penter(nc.psum_tensor("pC", [C, 1024], F32))
    pD = penter(nc.psum_tensor("pD", [C, 512], F32))
    pE = penter(nc.psum_tensor("pE", [C, 512], F32))

    s_dc = B.sem("d_c"); s_dw = B.sem("d_w")
    s_dx = [B.sem("d_x0"), B.sem("d_x1")]
    s_dxnd = B.sem("d_xnd"); s_dxnt = B.sem("d_xnt"); s_dktd = B.sem("d_ktd")
    s_dktok = [B.sem("d_ktok0"), B.sem("d_ktok1")]
    s_dod = [B.sem("d_od0"), B.sem("d_od1")]
    s_dot = [B.sem("d_ot0"), B.sem("d_ot1")]
    s_dout = [B.sem("d_out0"), B.sem("d_out1")]
    s_dxf = [B.sem("d_xf0"), B.sem("d_xf1")]
    s_dxn = [[B.sem("d_fxn00"), B.sem("d_fxn01")],
             [B.sem("d_fxn10"), B.sem("d_fxn11")]]
    s_pe = B.sem("pe"); s_act = B.sem("act"); s_dve = B.sem("dve")
    s_pool = B.sem("pool")

    SY, GP, PE, AC, DV, PO = "sync", "gpsimd", "pe", "act", "dve", "pool"

    def wsl(tsr, ft, c0, n, r0=0, nr=C):
        return tsr[r0:r0 + nr, W * ft + c0: W * ft + c0 + n]

    B.dma(SY, s_dc, maskS[:], mask_d)
    B.dma(SY, s_dc, eg_s[:], eg_d)
    B.dma(SY, s_dc, ones2_s[:], ones2_d)
    B.dma(SY, s_dc, bgv_s[:], bgv_d)
    B.dma(SY, s_dc, bsv_s[:], bsv_d)
    for ft in range(NT):
        for dst, src in ((bq_s, bq_d), (bk1_s, bk1_d), (bk2s_s, bk2_d),
                         (mur_s, mur_d), (muk_s, muk_d)):
            B.dma(SY, s_dc, dst[:, ft:ft + 1],
                  src[128 * ft:128 * ft + 128].rearrange("(p o) -> p o", o=1))
    c_consts = s_dc.cnt
    B.dma(GP, s_dw, wq_s.ap().rearrange("p (k d) -> p k d", d=D), wq_d.rearrange("(k p) d -> p k d", p=128))
    B.dma(GP, s_dw, wk1_s.ap().rearrange("p (k d) -> p k d", d=D), wk1_d.rearrange("(k p) d -> p k d", p=128))
    B.dma(GP, s_dw, wk2_s.ap().rearrange("p (k d) -> p k d", d=D), wk2_d.rearrange("(k p) d -> p k d", p=128))
    B.dma(GP, s_dw, wgb_s.ap().rearrange("p (f g) -> p f g", g=H), wgb_d.rearrange("(f p) g -> p f g", p=128))
    B.dma(GP, s_dw, wsb_s.ap().rearrange("p (f g) -> p f g", g=H), wsb_d.rearrange("(f p) g -> p f g", p=128))
    c_wqkk = s_dw.cnt
    B.dma(GP, s_dw, wp_s.ap().rearrange("p (k d) -> p k d", d=OF), wph_d.rearrange("(k p) d -> p k d", p=128))
    B.emit(GP, lambda e: e.memset(bp2[:], 0.0))
    B.dma(GP, s_dw, bp2[0:1, :], bph_d.rearrange("(o d) -> o d", o=1))
    c_winit = s_dw.cnt

    wc = [(s_dc, c_consts)]
    B.emit(PO, lambda e: e.memset(zrow[:], 0.0), waits=wc)
    B.emit(PO, lambda e: e.memset(ones16[:], 1.0))
    B.emit(PO, lambda e: e.memset(carry[:], 0.0))
    B.emit(PO, lambda e: e.memset(s1_acc[:], 0.0))
    for t in (qt_s, kb_s, kt_s, k2_s, q2_s, xnT):
        B.emit(PO, lambda e, t=t: e.memset(
            t.ap().rearrange("p (b w) -> p b w", w=W)[:, :, 0:LEAD], 0.0))
    B.emit(PO, lambda e: e.engine_nop(), incs=[(s_pool, 1)])
    c_prep_pool = s_pool.cnt
    B.emit(DV, lambda e: e.tensor_scalar_mul(omr_s[:], mur_s[:], -1.0), waits=wc)
    B.emit(DV, lambda e: e.tensor_scalar_add(omr_s[:], omr_s[:], 1.0))
    B.emit(DV, lambda e: e.tensor_scalar_mul(omk_s[:], muk_s[:], -1.0))
    B.emit(DV, lambda e: e.tensor_scalar_add(omk_s[:], omk_s[:], 1.0))
    B.emit(DV, lambda e: e.tensor_scalar_mul(bk2s_s[:], bk2s_s[:], float(nf32(0.02 / 32.0))))
    B.emit(DV, lambda e: e.tensor_scalar_mul(nbq8_s[:], bq_s[:], -0.125),
           incs=[(s_dve, 1)])
    B.dma(SY, s_dxnd, xnd[0:LEAD, :], zrow[:], waits=[(s_pool, c_prep_pool)])

    rec_pe_done = 0
    proj_pe_done = [0]
    ds1_mark = [0, 0]
    ds2_mark = [0, 0]
    sevac2_mark = [0, 0]
    s1cast_mark = [0, 0]
    pD_free_dve = 0
    ktokbuf_free = {0: 0, 1: 0}
    pA_free_dve = 0
    pB_free_dve = 0
    pE_free_act = 0
    pE_free_dve = 0
    bc_dve_free = [0]
    pD_free_act = 0
    o1sf_free = [(0, 0), (0, 0)]
    pms_free = {0: 0, 1: 0}
    p2m_free = {0: 0, 1: 0}
    od_cnt = {0: 0, 1: 0}
    pend_od = [None]
    xnd_cnt = 0
    c_ktS = 0
    qt_all = 0
    k1_evacs = {}

    xnd_holder = [0]
    wxnt_holder = [[]]

    ln_free = {0: 0, 1: 0}

    lnld = {0: 0, 1: 0}

    def emit_ln_loads2(s2, pair):
        sb0 = s2 * TSB
        for tt in (2 * pair, 2 * pair + 1):
            t0 = sb0 + C * tt
            xb = x_ln[tt % 2]
            xw = [(s_dve, ln_free[tt % 2])] if ln_free[tt % 2] else []
            lnld[tt % 2] = B.dma(SY, s_dx[tt % 2], xb[:], x_d[t0:t0 + C, :],
                                 waits=xw)

    def emit_ln_stats2(s2, pair):
        for tt in (2 * pair, 2 * pair + 1):
            xb = x_ln[tt % 2]
            wcx = [(s_dx[tt % 2], lnld[tt % 2])]
            B.emit(DV, lambda e, xb=xb, tt=tt: e.tensor_scalar(
                ln1[:], xb[:], 1.0, 0.0, ALU.mult, ALU.add,
                accum_out=stat_s1[:, tt:tt + 1]), waits=wcx)
            B.emit(DV, lambda e, xb=xb, tt=tt: e.scalar_tensor_tensor(
                ln2[:], xb[:], 1.0, xb[:], ALU.mult, ALU.mult,
                accum_out=stat_s2[:, tt:tt + 1]))
            B.emit(DV, lambda e, tt=tt: e.tensor_scalar_mul(
                stat_m[:, tt:tt + 1], stat_s1[:, tt:tt + 1], 1.0 / D))
            B.emit(DV, lambda e, tt=tt: e.tensor_scalar_mul(
                stat_v[:, tt:tt + 1], stat_s2[:, tt:tt + 1], 1.0 / D))
            B.emit(DV, lambda e, tt=tt: e.tensor_mul(
                stat_r[:, tt:tt + 1], stat_m[:, tt:tt + 1], stat_m[:, tt:tt + 1]))
            B.emit(DV, lambda e, tt=tt: e.tensor_sub(
                stat_v[:, tt:tt + 1], stat_v[:, tt:tt + 1], stat_r[:, tt:tt + 1]))
            xt2 = xtok[s2 % 2]
            xtw = [(s_pe, rec_pe_done)] if tt == 0 else []
            B.emit(DV, lambda e, xb=xb, xt2=xt2, tt=tt: e.tensor_scalar(
                xt2[:, D * tt:D * tt + D], xb[:], stat_m[:, tt:tt + 1], 1.0,
                ALU.subtract, ALU.mult),
                waits=xtw, incs=[(s_dve, 1)])
            ln_free[tt % 2] = s_dve.cnt

    def emit_ln_rstd_apply(s2):
        sb0 = s2 * TSB
        B.emit(AC, lambda e: e.activation(stat_r[:], stat_v[:], AF.Ln,
                                          bias=nf32(1e-5)),
               waits=[(s_dve, ln_free[0]), (s_dve, ln_free[1])])
        B.emit(AC, lambda e: e.activation(stat_r[:], stat_r[:], AF.Exp,
                                          scale=nf32(-0.5)), incs=[(s_act, 1)])
        cr = s_act.cnt
        xt = xtok[s2 % 2]
        for tt in range(CPS):
            B.emit(DV, lambda e, xt=xt, tt=tt: e.tensor_scalar_mul(
                xt[:, D * tt:D * tt + D], xt[:, D * tt:D * tt + D],
                stat_r[:, tt:tt + 1]),
                waits=[(s_act, cr)] if tt == 0 else [], incs=[(s_dve, 1)])
            cxn = s_dve.cnt
            t0 = sb0 + C * tt
            xnd_holder[0] = B.dma(SY, s_dxnd, xnd[LEAD + t0:LEAD + t0 + C, :],
                                  xt[:, D * tt:D * tt + D], waits=[(s_dve, cxn)])
    def emit_ln_treads(s2):
        sb0 = s2 * TSB
        trw = [(s_dxnd, xnd_holder[0]), (s_pe, proj_pe_done[0]),
               (s_dc, c_consts)]
        for ft in range(NT):
            B.dma(SY, s_dxnt, xnT[:, W * ft:W * ft + W],
                  xnd[sb0:sb0 + W, 128 * ft:128 * ft + 128],
                  waits=trw if ft == 0 else [], transpose=True)
        wxnt_holder[0] = [(s_dxnt, s_dxnt.cnt)]

    emit_ln_loads2(0, 0)
    emit_ln_stats2(0, 0)
    emit_ln_loads2(0, 1)
    emit_ln_stats2(0, 1)
    emit_ln_rstd_apply(0)
    emit_ln_treads(0)
    for s in range(NSB):
        sb0 = s * TSB

        gw = wxnt_holder[0] + [(s_dw, c_winit), (s_act, pD_free_act),
                               (s_dve, pD_free_dve)]
        for ft in range(NT):
            B.mm(lambda e, ft=ft: e.matmul(
                pD[0:H, 0:TSB], wgb_s[:, H * ft:H * ft + H],
                wsl(xnT, ft, LEAD, TSB), start=(ft == 0), stop=(ft == NT - 1)),
                0, 0,
                waits=gw if ft == 0 else [],
                incs=[(s_pe, 1)] if ft == NT - 1 else [])
        c_gpre = s_pe.cnt
        B.emit(AC, lambda e: e.activation(g_t1[:], pD[0:H, 0:TSB], AF.Sigmoid,
                                          bias=bgv_s[:]),
               waits=[(s_pe, c_gpre)], incs=[(s_act, 1)])
        c_g = s_act.cnt
        pD_free_act = c_g

        k1_evacs = {}
        for ft in range(NT):
            ph = pA[:, 512 * (ft % 2):512 * (ft % 2) + 512]
            if ft < 2:
                waits = [(s_dw, c_winit), (s_dve, pA_free_dve)]
            else:
                waits = [(s_act, k1_evacs[ft - 2])]
            for kt in range(NT):
                B.mm(lambda e, ph=ph, kt=kt, ft=ft: e.matmul(
                    ph, wk1_s[:, D * kt + 128 * ft: D * kt + 128 * ft + 128],
                    wsl(xnT, kt, LEAD, TSB), start=(kt == 0), stop=(kt == NT - 1)),
                    0, 0,
                    waits=waits if kt == 0 else [],
                    incs=[(s_pe, 1)] if kt == NT - 1 else [])
            cpe = s_pe.cnt
            B.emit(AC, lambda e, ph=ph, ft=ft: e.activation(
                wsl(kb_s, ft, LEAD, TSB), ph, AF.Identity, bias=bk1_s[:, ft:ft + 1]),
                waits=[(s_pe, cpe)], incs=[(s_act, 1)])
            k1_evacs[ft] = s_act.cnt
        c_kb = s_act.cnt
        c_k1pe = s_pe.cnt

        for ft in range(NT):
            B.mm(lambda e, ft=ft: e.matmul(
                pD[0:H, 0:TSB], wsb_s[:, H * ft:H * ft + H],
                wsl(kb_s, ft, LEAD, TSB), start=(ft == 0), stop=(ft == NT - 1)),
                0, 0,
                waits=[(s_act, c_kb)] if ft == 0 else [],
                incs=[(s_pe, 1)] if ft == NT - 1 else [])
        c_rpre = s_pe.cnt
        B.emit(AC, lambda e: e.activation(g_r[:], pD[0:H, 0:TSB], AF.Sigmoid,
                                          bias=bsv_s[:]),
               waits=[(s_pe, c_rpre)])
        B.emit(AC, lambda e: e.activation(g_t2[:], pD[0:H, 0:TSB], AF.Identity,
                                          bias=bsv_s[:]), incs=[(s_act, 1)])
        ca2 = s_act.cnt
        pD_free_act = ca2
        B.emit(DV, lambda e: e.tensor_mul(g_r[:], g_r[:], g_t2[:]),
               waits=[(s_act, ca2)], incs=[(s_dve, 1)])
        c_rsig = s_dve.cnt

        B.emit(DV, lambda e: e.tensor_scalar_max(g_t1[:], g_t1[:], 1e-6),
               waits=[(s_act, c_g)], incs=[(s_dve, 1)])
        cd = s_dve.cnt
        B.emit(AC, lambda e: e.activation(g_t2[:], g_t1[:], AF.Ln),
               waits=[(s_dve, cd)], incs=[(s_act, 1)])
        ca2 = s_act.cnt
        B.emit(DV, lambda e: e.tensor_tensor_scan(
            g_cum[:], ones16[:], g_t2[:], carry[:, 0:1], ALU.mult, ALU.add),
            waits=[(s_act, ca2), (s_pool, c_prep_pool)])
        B.emit(DV, lambda e: e.tensor_copy(carry[:], g_cum[:, TSB - 1:TSB]))
        B.emit(DV, lambda e: e.tensor_scalar_max(g_t1[:], g_cum[:], -30.0),
               incs=[(s_dve, 1)])
        cd = s_dve.cnt
        B.emit(AC, lambda e: e.activation(g_gc[:], g_t1[:], AF.Exp),
               waits=[(s_dve, cd)], incs=[(s_act, 1)])
        ca2 = s_act.cnt
        B.emit(DV, lambda e: e.tensor_scalar_add(g_gc[:], g_gc[:], 1e-6),
               waits=[(s_act, ca2)])
        B.emit(DV, lambda e: e.reciprocal(g_a[:], g_gc[:]), incs=[(s_dve, 1)])
        c_recip = s_dve.cnt
        B.emit(DV, lambda e: e.tensor_mul(g_arr[:], g_a[:], g_r[:]),
               waits=[(s_dve, c_rsig), (s_dve, c_recip)])
        B.emit(DV, lambda e: e.tensor_scalar_mul(g_gcbf[:], g_gc[:], 1.0))
        B.emit(DV, lambda e: e.tensor_scalar_mul(g_arrbf[:], g_arr[:], 1.0),
               incs=[(s_dve, 1)])
        c_gatebf = s_dve.cnt

        if s + 1 < NSB:
            emit_ln_loads2(s + 1, 0)
            emit_ln_stats2(s + 1, 0)
            emit_ln_loads2(s + 1, 1)
            emit_ln_stats2(s + 1, 1)

        k2_marks = {}
        for ft in range(NT):
            ph = pA[:, 512 * (ft % 2):512 * (ft % 2) + 512]
            if ft < 2:
                waits = [(s_act, k1_evacs[NT - 2 + ft])]
            else:
                waits = [(s_act, k2_marks[ft - 2])]
            for kt in range(NT):
                B.mm(lambda e, ph=ph, kt=kt, ft=ft: e.matmul(
                    ph, wk2_s[:, D * kt + 128 * ft: D * kt + 128 * ft + 128],
                    wsl(xnT, kt, LEAD, TSB), start=(kt == 0), stop=(kt == NT - 1)),
                    0, 0,
                    waits=waits if kt == 0 else [],
                    incs=[(s_pe, 1)] if kt == NT - 1 else [])
            ck = s_pe.cnt
            B.emit(AC, lambda e, ph=ph, ft=ft: e.activation(
                wsl(k2_s, ft, LEAD, TSB), ph, AF.Sigmoid,
                bias=bk2s_s[:, ft:ft + 1], scale=nf32(0.02 / 32.0)),
                waits=[(s_pe, ck)], incs=[(s_act, 1)])
            k2_marks[ft] = s_act.cnt
            if s == 0:
                B.dma(SY, s_dktd, k2d[128 * ft:128 * ft + 128, 0:TSB + 1],
                      wsl(k2_s, ft, LEAD - 1, TSB + 1),
                      waits=[(s_act, k2_marks[ft])])
            else:
                nk2 = min(TSB, L - (sb0 + 1))
                B.dma(SY, s_dktd, k2d[128 * ft:128 * ft + 128, sb0 + 1:sb0 + 1 + nk2],
                      wsl(k2_s, ft, LEAD, nk2), waits=[(s_act, k2_marks[ft])])
        c_k2 = s_act.cnt
        c_k2pe = s_pe.cnt

        for ft in range(NT):
            fp = ft % 2
            pX = pE if fp else pD
            bw = [(s_dve, c_gatebf)]
            if fp:
                bw += [(s_act, pE_free_act), (s_dve, pE_free_dve)]
            else:
                bw += [(s_act, pD_free_act), (s_dve, bc_dve_free[0])]
            B.mm(lambda e, ft=ft, pX=pX: e.matmul(
                pX[:, 0:TSB], eg_s[:, C * ft:C * ft + C], g_arrbf[:],
                start=True, stop=True),
                0, 0,
                waits=bw, incs=[(s_pe, 1)])
            cb = s_pe.cnt
            B.emit(AC, lambda e, fp=fp, pX=pX: e.copy(arrb_s[fp][:], pX[:, 0:TSB]),
                   waits=[(s_pe, cb)], incs=[(s_act, 1)])
            cab = s_act.cnt
            if fp:
                pE_free_act = cab
            else:
                pD_free_act = cab
            B.emit(DV, lambda e, ft=ft, fp=fp: e.tensor_mul(
                wsl(kt_s, ft, LEAD, TSB), wsl(kb_s, ft, LEAD, TSB), arrb_s[fp][:]),
                waits=[(s_act, cab)], incs=[(s_dve, 1)])
            ckt = s_dve.cnt
            if fp:
                pE_free_dve = ckt
            else:
                bc_dve_free[0] = ckt
            B.dma(SY, s_dktd, ktd[128 * ft:128 * ft + 128, sb0:sb0 + TSB],
                  wsl(kt_s, ft, LEAD, TSB), waits=[(s_dve, ckt)])
        qt_marks = {}
        for ft in range(NT):
            ph = pA[:, 512 * (ft % 2):512 * (ft % 2) + 512]
            if ft < 2:
                waits = [(s_act, k2_marks[NT - 2 + ft])]
            else:
                waits = [(s_dve, qt_marks[ft - 2][0]), (s_act, qt_marks[ft - 2][1])]
            for kt in range(NT):
                B.mm(lambda e, ph=ph, kt=kt, ft=ft: e.matmul(
                    ph, wq_s[:, D * kt + 128 * ft: D * kt + 128 * ft + 128],
                    wsl(xnT, kt, LEAD, TSB), start=(kt == 0), stop=(kt == NT - 1)),
                    0, 0,
                    waits=waits if kt == 0 else [],
                    incs=[(s_pe, 1)] if kt == NT - 1 else [])
            fp = ft % 2
            pX = pE if fp else pD
            bw = [(s_dve, c_gatebf)]
            if fp:
                bw += [(s_act, pE_free_act), (s_dve, pE_free_dve)]
            else:
                bw += [(s_act, pD_free_act), (s_dve, bc_dve_free[0])]
            B.mm(lambda e, ft=ft, pX=pX: e.matmul(
                pX[:, 0:TSB], eg_s[:, C * ft:C * ft + C], g_gcbf[:],
                start=True, stop=True),
                0, 0,
                waits=bw, incs=[(s_pe, 1)])
            cb = s_pe.cnt
            B.emit(AC, lambda e, fp=fp, pX=pX: e.copy(gcb_s[fp][:], pX[:, 0:TSB]),
                   waits=[(s_pe, cb)], incs=[(s_act, 1)])
            cgc = s_act.cnt
            if fp:
                pE_free_act = cgc
            else:
                pD_free_act = cgc
            cq = s_pe.cnt
            B.emit(DV, lambda e, ph=ph, ft=ft: e.tensor_scalar(
                q2scr[:], ph, bq_s[:, ft:ft + 1], -0.125,
                ALU.add, ALU.mult), waits=[(s_pe, cq)])
            B.emit(DV, lambda e, ft=ft: e.scalar_tensor_tensor(
                wsl(q2_s, ft, LEAD, TSB), q2scr[:], 0.02, q2scr[:],
                ALU.mult, ALU.max), incs=[(s_dve, 1)])
            cq2 = s_dve.cnt
            B.emit(DV, lambda e, ph=ph, ft=ft, fp=fp: e.scalar_tensor_tensor(
                wsl(qt_s, ft, LEAD, TSB), ph, bq_s[:, ft:ft + 1], gcb_s[fp][:],
                ALU.add, ALU.mult),
                waits=[(s_act, cgc)], incs=[(s_dve, 1)])
            if fp:
                pE_free_dve = s_dve.cnt
            else:
                bc_dve_free[0] = s_dve.cnt
            qt_marks[ft] = (s_dve.cnt, cgc)
        qt_all = qt_marks[NT - 1][0]

        ktd_cnt = s_dktd.cnt
        c_ktS = s_dve.cnt
        proj_pe_done[0] = s_pe.cnt
        early_ktok = {}
        for i0 in range(min(2, CPS)):
            c0 = CPS * s + i0
            p0 = c0 % 2
            ktw0 = [(s_dktd, ktd_cnt)]
            if c0 >= 2:
                ktw0.append((s_pe, ktokbuf_free[p0]))
            B.dma(AC, s_dktok[p0], ktok[p0][:], ktd[0:D, C * c0:C * c0 + C],
                  waits=ktw0, transpose=True)
            B.dma(AC, s_dktok[p0], k2tok[p0][:], k2d[0:D, C * c0:C * c0 + C],
                  transpose=True)
            early_ktok[c0] = s_dktok[p0].cnt
        chunk_hooks = {}
        if s + 1 < NSB:
            emit_ln_rstd_apply(s + 1)
            emit_ln_treads(s + 1)
        else:
            B.dma(GP, s_dw, wkt_s.ap().rearrange("p (k d) -> p k d", d=D),
                  wktf_d.rearrange("(k p) d -> p k d", p=128),
                  waits=[(s_pe, c_k1pe)])
            B.dma(GP, s_dw, wrt_s.ap().rearrange("p (k d) -> p k d", d=OF),
                  wrth_d.rearrange("(k p) d -> p k d", p=128),
                  waits=[(s_pe, c_k2pe)])
            B.dma(GP, s_dw, wvt_s.ap().rearrange("p (k d) -> p k d", d=OF),
                  wvth_d.rearrange("(k p) d -> p k d", p=128))
        if STOP <= 2:
            B.stopped = True

        for i in range(CPS):
            c = CPS * s + i
            par = c % 2
            if c in early_ktok:
                c_ktok = early_ktok[c]
            else:
                ktw = [(s_dktd, ktd_cnt)]
                if c >= 2:
                    ktw.append((s_pe, ktokbuf_free[par]))
                B.dma(AC, s_dktok[par], ktok[par][:], ktd[0:D, C * c:C * c + C],
                      waits=ktw, transpose=True)
                B.dma(AC, s_dktok[par], k2tok[par][:], k2d[0:D, C * c:C * c + C],
                      transpose=True)
                c_ktok = s_dktok[par].cnt
            if pend_od[0] is not None:
                pc, ppar, pcomb = pend_od[0]
                od_cnt[ppar] = B.dma(
                    SY, s_dod[ppar], od[C * pc + LEAD - 1:C * pc + LEAD - 1 + C, :],
                    ocomb[ppar][:], waits=[(s_dve, pcomb)])
                pend_od[0] = None
            c_k2z = 0
            if c == 0:
                B.emit(DV, lambda e: e.memset(k2tok[0][0:1, :], 0.0),
                       waits=[(s_dktok[par], c_ktok)], incs=[(s_dve, 1)])
                c_k2z = s_dve.cnt

            a0 = LEAD + C * i
            m1, m2 = a0 - 1, a0 - 2
            xt = xtok[s % 2]

            def kcut(nn, c=c):
                if c >= KCHUNK and nn >= KSUB:
                    B.stopped = True
            kcut(0)
            if c > 0:
                B.emit(AC, lambda e: e.copy(s2_sb[0:DH, 0:512], pC[0:DH, 0:512]),
                       waits=[(s_pe, ds2_mark[1])])
                B.emit(AC, lambda e: e.copy(s2_sb[DH:C, 512:1024], pC[DH:C, 512:1024]),
                       incs=[(s_act, 1)])
                sevac2_mark[0] = sevac2_mark[1] = s_act.cnt
            for g in range(2):
                hb = 8 * g

                def ps_block(par_jj, first_waits, inc_last):
                    for idx, jj in enumerate(par_jj):
                        j = hb + jj
                        ftj, rh = j // 2, DH * (j % 2)
                        B.mm(lambda e, jj=jj, ftj=ftj, rh=rh, a0=a0, m1=m1: e.matmul(
                            pA[:, 128 * jj:128 * jj + 128],
                            kt_s[rh:rh + DH, W * ftj + a0: W * ftj + a0 + C],
                            qt_s[rh:rh + DH, W * ftj + m1: W * ftj + m1 + C],
                            start=(jj % 4 == 0), stop=(jj % 4 == 3)),
                            rh, 0,
                            waits=first_waits if idx == 0 else [],
                            incs=[(s_pe, 1)] if (inc_last and idx == len(par_jj) - 1) else [])

                fw = []
                if c % CPS == 0 and g == 0:
                    fw = [(s_act, c_k2), (s_dve, qt_all), (s_dve, c_ktS)]
                fw.append((s_dve, pA_free_dve))
                ps_block((0, 2, 4, 6), fw, False)
                ps_block((1, 3, 5, 7), [], True)
                c_ps = s_pe.cnt
                kcut(2)

                def ds1_block(first_waits):
                    for idx, jj in enumerate((0, 2, 4, 6, 1, 3, 5, 7)):
                        j = hb + jj
                        rh2 = DH * (j % 2)
                        cl = 64 * (jj // 2) + 256 * (j % 2)
                        B.mm(lambda e, j=j, par=par, i=i, xt=xt, rh2=rh2,
                             cl=cl: e.matmul(
                            pD[rh2:rh2 + DH, cl:cl + DH],
                            ktok[par][:, DH * j:DH * j + DH],
                            xt[:, D * i + DH * j: D * i + DH * j + DH],
                            start=True, stop=True, skip_group_check=True),
                            0, rh2,
                            waits=first_waits if idx == 0 else [],
                            incs=[(s_pe, 1)] if idx == 7 else [])

                dw = [(s_dktok[par], c_ktok)]
                if c_k2z:
                    dw.append((s_dve, c_k2z))
                kcut(3)
                mb = (2 * c + g) % 2
                B.emit(DV, lambda e, mb=mb: e.tensor_mul(pms[mb][:, 0:512],
                                                         pA[:, 0:512], maskS[:]),
                       waits=[(s_pe, c_ps), (s_pe, pms_free[mb])])
                B.emit(DV, lambda e, mb=mb: e.tensor_mul(pms[mb][:, 512:1024],
                                                         pA[:, 512:1024], maskS[:]),
                       incs=[(s_dve, 1)])
                c_pm = s_dve.cnt
                kcut(4)

                def intra1_block():
                    for jj in range(8):
                        j = hb + jj
                        waits = []
                        if jj == 0:
                            waits = [(s_dve, c_pm), (s_act, pD_free_act),
                                     (s_dve, pD_free_dve)]
                        last = (jj == 7) and (c == 0)
                        B.mm(lambda e, jj=jj, j=j, i=i, xt=xt, last=last,
                             mb=mb: e.matmul(
                            pD[:, DH * jj:DH * jj + DH],
                            pms[mb][:, 128 * jj:128 * jj + 128],
                            xt[:, D * i + DH * j: D * i + DH * j + DH],
                            start=(jj == 0), stop=last),
                            0, 0,
                            waits=waits, incs=[(s_pe, 1)] if last else [])

                def inter1_block(par_jj, inc_last):
                    for idx, jj in enumerate(par_jj):
                        j = hb + jj
                        ftj, rh = j // 2, DH * (j % 2)
                        cl2 = 512 * (j % 2) + DH * (j // 2)
                        iw = [(s_dve, s1cast_mark[g])] if (idx == 0 and jj == 0) else []
                        B.mm(lambda e, jj=jj, ftj=ftj, rh=rh, m1=m1, cl2=cl2: e.matmul(
                            pD[:, DH * jj:DH * jj + DH],
                            qt_s[rh:rh + DH, W * ftj + m1: W * ftj + m1 + C],
                            s1_sb[rh:rh + DH, cl2:cl2 + DH],
                            start=False, stop=(jj == 7)),
                            rh, 0,
                            waits=iw,
                            incs=[(s_pe, 1)] if (inc_last and idx == len(par_jj) - 1) else [])

                def p2_block(par_jj, first_waits, inc_last):
                    for idx, jj in enumerate(par_jj):
                        j = hb + jj
                        ftj, rh = j // 2, DH * (j % 2)
                        B.mm(lambda e, jj=jj, ftj=ftj, rh=rh, m1=m1, m2=m2: e.matmul(
                            pB[:, 128 * jj:128 * jj + 128],
                            k2_s[rh:rh + DH, W * ftj + m1: W * ftj + m1 + C],
                            q2_s[rh:rh + DH, W * ftj + m2: W * ftj + m2 + C],
                            start=(jj % 4 == 0), stop=(jj % 4 == 3)),
                            rh, 0,
                            waits=first_waits if idx == 0 else [],
                            incs=[(s_pe, 1)] if (inc_last and idx == len(par_jj) - 1) else [])

                p2_block((0, 2, 4, 6), [(s_dve, pB_free_dve)], False)
                p2_block((1, 3, 5, 7), [], True)
                c_p2 = s_pe.cnt
                kcut(5)
                B.emit(DV, lambda e, mb=mb: e.tensor_mul(p2m[mb][:, 0:512],
                                                         pB[:, 0:512], maskS[:]),
                       waits=[(s_pe, c_p2), (s_pe, p2m_free[mb])])
                B.emit(DV, lambda e, mb=mb: e.tensor_mul(p2m[mb][:, 512:1024],
                                                         pB[:, 512:1024], maskS[:]),
                       incs=[(s_dve, 1)])
                c_p2m = s_dve.cnt
                pB_free_dve = c_p2m
                pA_free_dve = c_pm
                intra1_block()
                if c > 0:
                    inter1_block((0, 2, 4, 6), False)
                    inter1_block((1, 3, 5, 7), True)
                c_o1pe = s_pe.cnt
                pms_free[mb] = c_o1pe
                B.emit(AC, lambda e, g=g: e.copy(o1sf[:, 512 * g:512 * g + 512],
                                                 pD[:]),
                       waits=[(s_pe, c_o1pe), (s_pool, o1sf_free[g][0]),
                              (s_dve, o1sf_free[g][1])],
                       incs=[(s_act, 1)])
                c_o1ac = s_act.cnt
                pD_free_act = c_o1ac
                ew = [(s_act, c_o1ac)]
                if c > 0:
                    ew.append((s_pe, ds2_mark[g]))
                B.emit(PO, lambda e, g=g, i=i, xt=xt: e.tensor_sub(
                    e_buf[:, 512 * g:512 * g + 512],
                    xt[:, D * i + 512 * g: D * i + 512 * g + 512],
                    o1sf[:, 512 * g:512 * g + 512]),
                    waits=ew, incs=[(s_pool, 1)])
                c_e = s_pool.cnt
                kcut(6)
                ds1_block(dw + [(s_act, c_o1ac)])
                ds1_mark[g] = s_pe.cnt
                B.emit(DV, lambda e, g=g: e.tensor_add(
                    s1_acc[0:DH, 256 * g:256 * g + 256],
                    pD[0:DH, 0:256], s1_acc[0:DH, 256 * g:256 * g + 256]),
                    waits=[(s_pe, ds1_mark[g])])
                B.emit(DV, lambda e, g=g: e.tensor_add(
                    s1_acc[DH:C, 512 + 256 * g:768 + 256 * g],
                    pD[DH:C, 256:512],
                    s1_acc[DH:C, 512 + 256 * g:768 + 256 * g]),
                    incs=[(s_dve, 1)])
                pD_free_dve = s_dve.cnt
                B.emit(DV, lambda e, g=g: e.tensor_scalar_mul(
                    s1_sb[0:DH, 256 * g:256 * g + 256],
                    s1_acc[0:DH, 256 * g:256 * g + 256], 1.0))
                B.emit(DV, lambda e, g=g: e.tensor_scalar_mul(
                    s1_sb[DH:C, 512 + 256 * g:768 + 256 * g],
                    s1_acc[DH:C, 512 + 256 * g:768 + 256 * g], 1.0),
                    incs=[(s_dve, 1)])
                s1cast_mark[g] = s_dve.cnt
                kcut(7)
                for jj in range(8):
                    j = hb + jj
                    waits = []
                    if jj == 0:
                        waits = [(s_dve, c_p2m), (s_pool, c_e),
                                 (s_act, pE_free_act), (s_dve, pE_free_dve)]
                    last = (jj == 7) and (c == 0)
                    B.mm(lambda e, jj=jj, j=j, last=last, mb=mb: e.matmul(
                        pE[:, DH * jj:DH * jj + DH],
                        p2m[mb][:, 128 * jj:128 * jj + 128],
                        e_buf[:, DH * j:DH * j + DH],
                        start=(jj == 0), stop=last),
                        0, 0,
                        waits=waits, incs=[(s_pe, 1)] if last else [])

                def inter2_block(par_jj, first_waits, inc_last):
                    for idx, jj in enumerate(par_jj):
                        j = hb + jj
                        ftj, rh = j // 2, DH * (j % 2)
                        cl2 = 512 * (j % 2) + DH * (j // 2)
                        B.mm(lambda e, jj=jj, ftj=ftj, rh=rh, m2=m2, cl2=cl2: e.matmul(
                            pE[:, DH * jj:DH * jj + DH],
                            q2_s[rh:rh + DH, W * ftj + m2: W * ftj + m2 + C],
                            s2_sb[rh:rh + DH, cl2:cl2 + DH],
                            start=False, stop=(jj == 7)),
                            rh, 0,
                            waits=first_waits if idx == 0 else [],
                            incs=[(s_pe, 1)] if (inc_last and idx == len(par_jj) - 1) else [])

                def ds2_block(par_jj, first_waits, inc_last):
                    for idx, jj in enumerate(par_jj):
                        j = hb + jj
                        rh2, cl2 = DH * (j % 2), 512 * (j % 2) + DH * (j // 2)
                        st_, sp_ = (c == 0 and j <= 1), (c == NCH - 1 and j >= H - 2)
                        B.mm(lambda e, j=j, par=par, rh2=rh2, cl2=cl2, st_=st_,
                             sp_=sp_: e.matmul(
                            pC[rh2:rh2 + DH, cl2:cl2 + DH],
                            k2tok[par][:, DH * j:DH * j + DH],
                            e_buf[:, DH * j:DH * j + DH],
                            start=st_, stop=sp_, skip_group_check=True),
                            0, rh2,
                            waits=first_waits if idx == 0 else [],
                            incs=[(s_pe, 1)] if (inc_last and idx == len(par_jj) - 1) else [])

                dw2 = [(s_pool, c_e)]
                if c > 0:
                    dw2.append((s_act, sevac2_mark[g]))
                if c == 0:
                    c_o2pe = s_pe.cnt
                if c > 0:
                    inter2_block((0, 2, 4, 6), [], False)
                ds2_block((0, 2, 4, 6), dw2, False)
                if c > 0:
                    inter2_block((1, 3, 5, 7), [], True)
                    c_o2pe = s_pe.cnt
                p2m_free[mb] = c_o2pe
                kcut(8)
                ds2_block((1, 3, 5, 7), [], True)
                ds2_mark[g] = s_pe.cnt
                kcut(9)
                if g == 1:
                    ktokbuf_free[par] = s_pe.cnt
                cw = [(s_pe, c_o2pe), (s_act, c_o1ac), (s_dod[par], od_cnt[par])]
                B.emit(DV, lambda e, g=g, par=par: e.tensor_sub(
                    ocomb[par][:, 512 * g:512 * g + 512],
                    o1sf[:, 512 * g:512 * g + 512], pE[:]),
                    waits=cw, incs=[(s_dve, 1)])
                c_comb = s_dve.cnt
                pE_free_dve = c_comb
                o1sf_free[g] = (c_e, c_comb)
            pend_od[0] = (c, par, c_comb)
            if i in chunk_hooks:
                chunk_hooks[i]()
        rec_pe_done = s_pe.cnt
        if STOP <= 3 and s == 0:
            B.stopped = True
        if s < NSB - 1:
            for t in (qt_s, q2_s, k2_s):
                B.emit(PO, lambda e, t=t: e.tensor_copy(
                    t.ap().rearrange("p (b w) -> p b w", w=W)[:, :, 0:LEAD],
                    t.ap().rearrange("p (b w) -> p b w", w=W)[:, :, TSB:TSB + LEAD]),
                    waits=[(s_pe, rec_pe_done)], incs=[(s_pool, 1)])
            c_lead = s_pool.cnt
            pA_free_dve = max(pA_free_dve, 0)
            B.emit(PE, lambda e: e.nop(), waits=[(s_pool, c_lead)])

    if STOP <= 4:
        B.stopped = True
    if pend_od[0] is not None:
        pc, ppar, pcomb = pend_od[0]
        od_cnt[ppar] = B.dma(
            SY, s_dod[ppar], od[C * pc + LEAD - 1:C * pc + LEAD - 1 + C, :],
            ocomb[ppar][:], waits=[(s_dve, pcomb)])
        pend_od[0] = None
    for g in range(2):
        B.emit(AC, lambda e, g=g: e.copy(
            s2_sb[0:DH, 256 * g:256 * g + 256], pC[0:DH, 256 * g:256 * g + 256]),
            waits=[(s_pe, ds2_mark[g])])
        B.emit(AC, lambda e, g=g: e.copy(
            s2_sb[DH:C, 512 + 256 * g:768 + 256 * g],
            pC[DH:C, 512 + 256 * g:768 + 256 * g]), incs=[(s_act, 1)])
    c_sfin = s_act.cnt
    for idx, j in enumerate([jv for jv in range(0, H, 2)] + [jv for jv in range(1, H, 2)]):
        ftj, rh = j // 2, DH * (j % 2)
        waits = [(s_act, c_sfin), (s_dve, pA_free_dve),
                 (s_dve, s1cast_mark[0]), (s_dve, s1cast_mark[1])] if idx == 0 else []
        cl2 = 512 * (j % 2) + DH * (j // 2)
        B.mm(lambda e, j=j, ftj=ftj, rh=rh, cl2=cl2: e.matmul(
            pA[0:1, DH * j:DH * j + DH],
            qt_s[rh:rh + DH, W * ftj + LEAD + 511: W * ftj + LEAD + 512],
            s1_sb[rh:rh + DH, cl2:cl2 + DH], start=(j % 8 == 0), stop=(j % 8 == 7),
            skip_group_check=True),
            rh, 0,
            waits=waits)
    for idx, j in enumerate([jv for jv in range(0, H, 2)] + [jv for jv in range(1, H, 2)]):
        ftj, rh = j // 2, DH * (j % 2)
        waits = [(s_dve, pE_free_dve), (s_act, pE_free_act),
                 (s_dve, pB_free_dve), (s_act, c_sfin)] if idx == 0 else []
        cl2 = 512 * (j % 2) + DH * (j // 2)
        B.mm(lambda e, j=j, ftj=ftj, rh=rh, cl2=cl2: e.matmul(
            pB[0:1, DH * j:DH * j + DH],
            q2_s[rh:rh + DH, W * ftj + LEAD + 510: W * ftj + LEAD + 511],
            s2_sb[rh:rh + DH, cl2:cl2 + DH], start=(j % 8 == 0), stop=(j % 8 == 7),
            skip_group_check=True),
            rh, 0,
            waits=waits, incs=[(s_pe, 1)] if idx == H - 1 else [])
    c_mini = s_pe.cnt
    B.emit(AC, lambda e: e.copy(minis[:], pA[0:1, 0:1024]),
           waits=[(s_pe, c_mini)], incs=[(s_act, 1)])
    c_mac = s_act.cnt
    B.emit(DV, lambda e: e.tensor_sub(ocomb[0][0:1, :], minis[:], pB[0:1, 0:1024]),
           waits=[(s_act, c_mac), (s_dod[0], s_dod[0].cnt)], incs=[(s_dve, 1)])
    c_mdv = s_dve.cnt
    B.dma(SY, s_dod[0], od[L + LEAD - 1:L + LEAD, :], ocomb[0][0:1, :],
          waits=[(s_dve, c_mdv)])
    c_od_all = (s_dod[0].cnt, s_dod[1].cnt)

    if STOP <= 5:
        B.stopped = True
    c_wffn = s_dw.cnt
    for kt in range(NT):
        B.emit(DV, lambda e, kt=kt: e.tensor_scalar_mul(
            wrA_s[:, OF * kt:OF * kt + OF], wrt_s[:, OF * kt:OF * kt + OF],
            mur_s[:, kt:kt + 1]),
            waits=[(s_dw, c_wffn)] if kt == 0 else [])
        B.emit(DV, lambda e, kt=kt: e.tensor_scalar_mul(
            wrB_s[:, OF * kt:OF * kt + OF], wrt_s[:, OF * kt:OF * kt + OF],
            omr_s[:, kt:kt + 1]),
            incs=[(s_dve, 1)] if kt == NT - 1 else [])
    c_wfold = s_dve.cnt

    ffn = dict(fread={}, fread_ft={}, cmix={}, kproj={}, cka={},
               rproj={}, vproj={}, sproj={},
               out_st={0: 0, 1: 0}, out_dve={0: 0, 1: 0},
               prev_r={0: 0, 1: 0}, prev_v={0: 0, 1: 0}, prev_s={0: 0, 1: 0})

    def ffn_head(q):
        b = q % 2
        tb = TF * q
        if q == 0:
            fw = [(s_dxnd, s_dxnd.cnt), (s_pool, s_pool.cnt),
                  (s_pe, c_mini), (s_act, c_mac), (s_dve, c_mdv)]
            ow = [(s_dod[0], c_od_all[0]), (s_dod[1], c_od_all[1]),
                  (s_pe, c_mini), (s_act, c_mac), (s_dve, c_mdv)]
        else:
            fw = [(s_pe, ffn["rproj"][q - 2])] if q >= 2 else []
            ow = [(s_pe, ffn["sproj"][q - 2])] if q >= 2 else []
        fts = []
        for half in range(2):
            for i in range(4):
                ft = 4 * half + i
                c = B.dma(SY, s_dxn[b][half], xnF_b[b][:, WF * ft:WF * ft + WF],
                          xnd[tb:tb + WF, 128 * ft:128 * ft + 128],
                          waits=fw if ft == 0 else [], transpose=True)
            fts.append((s_dxn[b][half], c))
        ffn["fread_ft"][q] = fts
        for ft in range(NT):
            B.dma(SY, s_dot[b], ot_s[b][:, TF * ft:TF * ft + TF],
                  od[tb + LEAD:tb + LEAD + TF, 128 * ft:128 * ft + 128],
                  waits=ow if ft == 0 else [], transpose=True)
        ffn["fread"][q] = (s_dot[b], s_dot[b].cnt)

    def ffn_mix(q):
        b = q % 2
        xnq = xnF_b[b]
        mu_c, om_c, dst = muk_s, omk_s, xk_s
        war = (s_pe, ffn["kproj"][q - 1]) if q >= 1 else None
        for ft in range(NT):
            fwm = [ffn["fread_ft"][q][ft // 4]] if ft % 4 == 0 else []
            if ft == 0 and war:
                fwm.append(war)
            B.emit(DV, lambda e, ft=ft, om_c=om_c, xnq=xnq: e.tensor_scalar_mul(
                kr2_s[:], xnq[:, WF * ft + LEAD - 1: WF * ft + LEAD - 1 + TF],
                om_c[:, ft:ft + 1]), waits=fwm)
            B.emit(DV, lambda e, ft=ft, mu_c=mu_c, dst=dst, xnq=xnq: e.scalar_tensor_tensor(
                dst[:, TF * ft:TF * ft + TF],
                xnq[:, WF * ft + LEAD: WF * ft + LEAD + TF],
                mu_c[:, ft:ft + 1], kr2_s[:], ALU.mult, ALU.add),
                incs=[(s_dve, 1)])
        ffn["cmix"][q] = s_dve.cnt

    def ffn_kproj(q):
        kevac = {}
        for m in range(NT):
            ph = pA[:, 512 * (m % 2):512 * (m % 2) + 512]
            if m < 2:
                waits = [(s_dve, ffn["cmix"][q]), (s_dw, c_wffn)]
                if q == 0:
                    waits += [(s_dve, pA_free_dve), (s_act, c_mac),
                              (s_dve, c_mdv)]
                else:
                    waits.append((s_dve, ffn["cka"][q - 1]))
            else:
                waits = [(s_dve, kevac[m - 2])]
            for kt in range(NT):
                B.mm(lambda e, ph=ph, kt=kt, m=m: e.matmul(
                    ph, wkt_s[:, D * kt + 128 * m: D * kt + 128 * m + 128],
                    xk_s[:, TF * kt: TF * kt + TF],
                    start=(kt == 0), stop=(kt == NT - 1)),
                    0, 0,
                    waits=waits if kt == 0 else [],
                    incs=[(s_pe, 1)] if kt == NT - 1 else [])
            ckp = s_pe.cnt
            krw = [(s_pe, ckp)]
            if m >= 1:
                krw.append((s_dve, kevac[m - 1]))
            B.emit(AC, lambda e, ph=ph: e.activation(
                kr_s[:], ph, AF.Relu),
                waits=krw, incs=[(s_act, 1)])
            cka = s_act.cnt
            kaw = [(s_act, cka)]
            if m == 0 and q >= 1:
                kaw.append((s_pe, ffn["vproj"][q - 1]))
            B.emit(DV, lambda e, m=m: e.tensor_mul(
                ka_s[:, TF * m: TF * m + TF],
                kr_s[:], kr_s[:]),
                waits=kaw, incs=[(s_dve, 1)])
            kevac[m] = s_dve.cnt
        ffn["cka"][q] = s_dve.cnt
        ffn["kproj"][q] = s_pe.cnt

    def ffn_tiles(q, hooks=None):
        b = q % 2
        tb = TF * q
        prev_r, prev_v, prev_s = ffn["prev_r"], ffn["prev_v"], ffn["prev_s"]
        for ttl in range(4):
            if hooks and ttl in hooks:
                hooks[ttl]()
            pr = ttl % 2
            ob = ttl % 2
            xb = x_ln[ob]
            cx = B.dma(GP, s_dxf[ob], xb[:, 0:OF],
                       xh_d[tb + 128 * ttl:tb + 128 * ttl + 128, :],
                       waits=[(s_dve, ffn["out_dve"][ob] or ln_free[ob])])
            waits0 = [(s_dve, c_wfold)]
            if ttl == 0:
                waits0 += list(ffn["fread_ft"][q])
            if prev_r[pr]:
                waits0.append((s_act, prev_r[pr]))
            else:
                waits0 += [(s_dve, c_mdv), (s_act, c_sfin)]
            for half in range(2):
                wsrc = wrA_s if half == 0 else wrB_s
                sh = LEAD if half == 0 else LEAD - 1
                for kt in range(NT):
                    st_ = (half == 0 and kt == 0)
                    sp_ = (half == 1 and kt == NT - 1)
                    B.mm(lambda e, kt=kt, ttl=ttl, pr=pr, b=b, wsrc=wsrc,
                         sh=sh, st_=st_, sp_=sp_: e.matmul(
                        pB[:, 512 * pr:512 * pr + 512],
                        xnF_b[b][:, WF * kt + sh + 128 * ttl: WF * kt + sh + 128 * ttl + 128],
                        wsrc[:, OF * kt: OF * kt + OF],
                        start=st_, stop=sp_),
                        0, 0,
                        waits=waits0 if st_ else [],
                        incs=[(s_pe, 1)] if sp_ else [])
            crp = s_pe.cnt
            if ttl >= 2:
                ffn["rproj"][q] = crp
            B.emit(AC, lambda e, pr=pr: e.activation(
                r_bf[pr][:], pB[:, 512 * pr:512 * pr + 512], AF.Sigmoid),
                   waits=[(s_pe, crp)], incs=[(s_act, 1)])
            prev_r[pr] = s_act.cnt
            waits0 = [(s_dve, ffn["cka"][q])]
            if prev_v[pr]:
                waits0.append((s_dve, prev_v[pr]))
            elif q == 0:
                waits0.append((s_act, c_sfin))
            for kt in range(NT):
                B.mm(lambda e, kt=kt, ttl=ttl, pr=pr: e.matmul(
                    pC[:, 512 * pr:512 * pr + 512],
                    ka_s[:, TF * kt + 128 * ttl: TF * kt + 128 * ttl + 128],
                    wvt_s[:, OF * kt: OF * kt + OF],
                    start=(kt == 0), stop=(kt == NT - 1)),
                    0, 0,
                    waits=waits0 if kt == 0 else [],
                    incs=[(s_pe, 1)] if kt == NT - 1 else [])
            cvp = s_pe.cnt
            if ttl >= 2:
                ffn["vproj"][q] = cvp
            ps_t = pD if pr == 0 else pE
            waits0 = [ffn["fread"][q], (s_dw, c_winit)]
            if prev_s[pr]:
                waits0.append((s_dve, prev_s[pr]))
            else:
                waits0 += [(s_act, c_mac), (s_dve, c_mdv), (s_act, pD_free_act),
                           (s_dve, pE_free_dve)]
            for kt in range(NT):
                B.mm(lambda e, kt=kt, ttl=ttl, ps_t=ps_t, b=b: e.matmul(
                    ps_t[:, 0:512],
                    ot_s[b][:, TF * kt + 128 * ttl: TF * kt + 128 * ttl + 128],
                    wp_s[:, OF * kt: OF * kt + OF],
                    start=(kt == 0), stop=False),
                    0, 0,
                    waits=waits0 if kt == 0 else [])
            B.mm(lambda e, ps_t=ps_t: e.matmul(
                ps_t[:, 0:512], ones2_s[:], bp2[:], start=False, stop=True),
                0, 0,
                incs=[(s_pe, 1)])
            csp = s_pe.cnt
            if ttl >= 2:
                ffn["sproj"][q] = csp
            B.emit(DV, lambda e, pr=pr: e.tensor_mul(
                gg1[pr][:], r_bf[pr][:], pC[:, 512 * pr:512 * pr + 512]),
                   waits=[(s_pe, cvp), (s_act, prev_r[pr]),
                          (s_dout[pr], ffn["out_st"][pr])], incs=[(s_dve, 1)])
            prev_v[pr] = s_dve.cnt
            cg1 = s_dve.cnt
            B.emit(AC, lambda e, pr=pr: e.activation(gg2[pr][:], gg1[pr][:],
                                                     AF.Sigmoid),
                   waits=[(s_dve, cg1)], incs=[(s_act, 1)])
            cg2 = s_act.cnt
            B.emit(DV, lambda e, pr=pr: e.tensor_mul(gg2[pr][:], gg2[pr][:],
                                                     gg1[pr][:]),
                   waits=[(s_act, cg2)])
            B.emit(DV, lambda e, pr=pr, ps_t=ps_t: e.tensor_mul(
                gg1[pr][:], ps_t[:, 0:512], gg2[pr][:]),
                   waits=[(s_pe, csp)], incs=[(s_dve, 1)])
            prev_s[pr] = s_dve.cnt
            B.emit(DV, lambda e, xb=xb, pr=pr, ob=ob, cx=cx: e.tensor_add(
                gg1[pr][:], gg1[pr][:], xb[:, 0:OF]),
                waits=[(s_dxf[ob], cx)],
                incs=[(s_dve, 1)])
            cfin = s_dve.cnt
            ffn["out_dve"][ob] = cfin
            ffn["out_st"][ob] = B.dma(
                GP, s_dout[ob], out_d[tb + 128 * ttl:tb + 128 * ttl + 128, :],
                gg1[pr][:], waits=[(s_dve, cfin)])

    ffn_head(0)
    ffn_mix(0)
    for q in range(NSB):
        if q + 1 < NSB:
            ffn_head(q + 1)
        ffn_kproj(q)
        if q + 1 < NSB:
            ffn_tiles(q, hooks={1: lambda q=q: ffn_mix(q + 1)})
        else:
            ffn_tiles(q)
    B.stopped = False
    _alls = [s_dc, s_dw, s_dxnd, s_dxnt, s_dktd] + s_dot + s_dx + s_dxf + s_dxn[0] + s_dxn[1] + s_dktok + s_dod + s_dout
    B.emit(SY, lambda e: e.nop(),
           waits=[(sm, sm.cnt) for sm in _alls])
    B.emit(GP, lambda e: e.engine_nop(),
           waits=[(s_dw, s_dw.cnt)])

    semctx = []
    for s in B.sems.values():
        cm = nc.semaphore(s.name)
        semctx.append(cm)
        s.h = cm.__enter__()
    with nc.Block() as block:
        def run(eng_name):
            names = ("gpsimd", "pool") if eng_name == "gpsimd" else (eng_name,)
            use_drain = KDRAIN and eng_name in ("act", "dve", "gpsimd")
            def body(e):
                first = True
                for eng, waits, fn, incs in B.steps:
                    if eng not in names:
                        continue
                    for sem, val in waits:
                        e.wait_ge(sem.h, val)
                    if use_drain and not first:
                        e.drain()
                    first = False
                    inst = fn(e)
                    for sem, d in incs:
                        inst.then_inc(sem.h, d)
            return body

        block.tensor(run("pe"))
        block.scalar(run("act"))
        block.vector(run("dve"))
        block.gpsimd(run("gpsimd"))
        block.sync(run("sync"))

    return nc


_NC_CACHE = {}


def kernel(**inputs):
    ins = {k: np.asarray(v) for k, v in inputs.items()}
    x = ins["L0_attn_x"].astype(np.float32)
    Wq = np.ascontiguousarray(ins["L0_attn_Wq"], np.float32)
    bq = np.ascontiguousarray(ins["L0_attn_bq"], np.float32)
    Wk1 = np.ascontiguousarray(ins["L0_attn_Wk1"], np.float32)
    bk1 = np.ascontiguousarray(ins["L0_attn_bk1"], np.float32)
    Wk2 = np.ascontiguousarray(ins["L0_attn_Wk2"], np.float32)
    bk2 = np.ascontiguousarray(ins["L0_attn_bk2"], np.float32)
    Wg = np.ascontiguousarray(ins["L0_attn_Wg"], np.float32)
    bg = np.ascontiguousarray(ins["L0_attn_bg"], np.float32)
    Ws = np.ascontiguousarray(ins["L0_attn_Ws"], np.float32)
    bs = np.ascontiguousarray(ins["L0_attn_bs"], np.float32)
    Wp = np.ascontiguousarray(ins["L0_attn_Wp"], np.float32)
    bp = np.ascontiguousarray(ins["L0_attn_bp"], np.float32)
    WrT = np.ascontiguousarray(ins["L0_ffn_W_r"].T, np.float32)
    WkT = np.ascontiguousarray(ins["L0_ffn_W_k"].T, np.float32)
    WvT = np.ascontiguousarray(ins["L0_ffn_W_v"].T, np.float32)
    mu_r = np.ascontiguousarray(ins["L0_ffn_mu_r"], np.float32)
    mu_k = np.ascontiguousarray(ins["L0_ffn_mu_k"], np.float32)

    wgb = np.zeros((D, H), np.float32)
    wsb = np.zeros((D, H), np.float32)
    for j in range(H):
        wgb[64 * j:64 * j + 64, j] = Wg[:, 0]
        wsb[64 * j:64 * j + 64, j] = Ws[:, 0]
    bgv = np.repeat(bg[0], H).reshape(H, 1).astype(np.float32)
    bsv = np.repeat(bs[0], H).reshape(H, 1).astype(np.float32)
    maskS = np.tile(np.triu(np.ones((C, C), np.float32), 1), (1, 4)).astype(ml_dtypes.bfloat16)
    EG = np.zeros((H, NT * C), np.float32)
    for j in range(H):
        EG[j, C * (j // 2) + 64 * (j % 2): C * (j // 2) + 64 * (j % 2) + 64] = 1.0
    EG = EG.astype(ml_dtypes.bfloat16)
    ones2 = np.zeros((2, C), np.float32)
    ones2[0, :] = 1.0
    ones2 = ones2.astype(ml_dtypes.bfloat16)

    nc = build_nc()

    in_maps = []
    for c in range(8):
        b, p = c // 2, c % 2
        sl = slice(OF * p, OF * p + OF)
        m = dict(
            x=np.ascontiguousarray(x[b]),
            xh=np.ascontiguousarray(x[b][:, sl]),
            Wq=Wq, Wk1=Wk1, Wk2=Wk2,
            WpH=np.ascontiguousarray(Wp[:, sl]),
            WrTH=np.ascontiguousarray(WrT[:, sl]),
            WkT=WkT,
            WvTH=np.ascontiguousarray(WvT[:, sl]),
            bq=bq, bk1=bk1, bk2=bk2,
            bpH=np.ascontiguousarray(bp[sl]),
            mu_r=mu_r, mu_k=mu_k, wgb=wgb, wsb=wsb, bgv=bgv, bsv=bsv,
            maskS=maskS, EGF=EG, ones2=ones2,
        )
        in_maps.append(m)

    res = run_bass_kernel_spmd(nc, in_maps, core_ids=list(range(8)))
    out = np.empty((4, L, D), np.float32)
    for c in range(8):
        b, p = c // 2, c % 2
        out[b, :, OF * p:OF * p + OF] = np.asarray(res.results[c]["out"])
    return out
